# version 03
# baseline (speedup 1.0000x reference)
import numpy as np
from contextlib import ExitStack
import concourse.bass as bass
import concourse.mybir as mybir
from concourse.bass_utils import run_bass_kernel_spmd

F32 = mybir.dt.float32
BF = mybir.dt.bfloat16
AF = mybir.ActivationFunctionType
OP = mybir.AluOpType
AX = mybir.AxisListType.X

T = 4096
D = 1024
TB = 512
NB = T // TB
NEG = -1.0e30
EPS = 1e-5


def _host_consts():
    c = {}
    half = np.arange(0, 64, 2, dtype=np.float64)
    inv_freq = 1.0 / (10000.0 ** (half / 64.0))
    t = np.arange(T, dtype=np.float64)
    freqs = t[:, None] * inv_freq[None, :]
    freqs = np.concatenate([freqs, freqs], -1)
    base_scale = (half + 0.4 * 64.0) / (1.4 * 64.0)
    power = (t - T // 2) / (256 // 2)
    scale = base_scale[None, :] ** power[:, None]
    scale = np.concatenate([scale, scale], -1)
    cos = np.cos(freqs)
    sin = np.sin(freqs)
    for nm, arr in (("CS", cos * scale), ("SS", sin * scale),
                    ("CI", cos / scale), ("SI", sin / scale)):
        a = np.ascontiguousarray(np.tile(arr.T, (2, 1)).astype(np.float32))
        c["c_" + nm] = a
    masks = np.zeros((128, 4 * 512), dtype=np.float32)
    for p in range(2):
        i = np.arange(128)[:, None]
        jj = np.arange(512)[None, :]
        lo = 128 * p + i
        keep = (jj >= lo) & (jj <= lo + 256)
        m = np.where(keep, 0.0, NEG).astype(np.float32)
        masks[:, p * 512:(p + 1) * 512] = m
        m0 = m.copy()
        m0[:, :256] = NEG
        masks[:, (2 + p) * 512:(3 + p) * 512] = m0
    c["c_masks"] = masks
    j = np.arange(128)[:, None]
    i = np.arange(128)[None, :]
    c["c_ltm"] = np.where(i >= j, 0.0, NEG).astype(np.float32)
    c["c_ut"] = np.where(j <= i, 1.0, 0.0).astype(np.float32)
    c["c_if"] = np.eye(128, dtype=np.float32)
    import ml_dtypes
    c["c_ib"] = np.eye(128, dtype=np.float32).astype(ml_dtypes.bfloat16)
    id2 = np.vstack([np.eye(64), np.eye(64)]).astype(np.float32)
    c["c_id2f"] = id2
    c["c_id2b"] = id2.astype(ml_dtypes.bfloat16)
    c["c_onesf"] = np.ones((128, 512), dtype=np.float32)
    c["c_onesb"] = np.ones((128, 512), dtype=np.float32).astype(ml_dtypes.bfloat16)
    s2 = np.zeros((128, 2), dtype=np.float32)
    s2[:64, 0] = 1.0
    s2[64:, 1] = 1.0
    c["c_sel2b"] = s2.astype(ml_dtypes.bfloat16)
    c["c_sel2tb"] = np.ascontiguousarray(s2.T).astype(ml_dtypes.bfloat16)
    R = np.zeros((128, 128), dtype=np.float32)
    for d_ in range(128):
        m = d_ % 64
        b0 = d_ - m
        if m < 32:
            R[d_, b0 + m + 32] = -1.0
        else:
            R[d_, b0 + m - 32] = 1.0
    c["c_rht"] = np.ascontiguousarray(R.T).astype(ml_dtypes.bfloat16)
    c["c_scal"] = np.tile(np.array([1e-5, 1.0, -1.0, 0.0, 0.5, 8.0, 1e-12, 2.0], dtype=np.float32), (128, 1))
    return c


class K:
    def __init__(self, nc):
        self.nc = nc
        self.sem = {e: nc.alloc_semaphore("S" + e) for e in "tvsd"}
        self.cnt = {e: 0 for e in "tvsd"}
        self.wm = {e: {o: 0 for o in "tvsd"} for e in "tvsd"}
        self.buf = {}
        self.q = {e: [] for e in "tvsd"}
        self.nstage = 0

    def _key(self, ap):
        t = getattr(ap, "tensor", None)
        if t is None:
            return str(ap)
        return t.name

    def emit(self, eng, fn, outs, ins):
        need = {}
        for ap in ins:
            rec = self.buf.get(self._key(ap))
            if rec and rec["w"]:
                we, wc = rec["w"]
                if not (we == eng == "t"):
                    need[we] = max(need.get(we, 0), wc)
        for ap in outs:
            rec = self.buf.get(self._key(ap))
            if rec:
                if rec["w"]:
                    we, wc = rec["w"]
                    if not (we == eng == "t"):
                        need[we] = max(need.get(we, 0), wc)
                for re_, rc in rec["r"].items():
                    if re_ != eng and not (re_ == eng == "t"):
                        need[re_] = max(need.get(re_, 0), rc)
                    elif re_ == eng == "d":
                        need[re_] = max(need.get(re_, 0), rc)
        for oe in "tvsd":
            if oe in need and self.wm[eng][oe] < need[oe]:
                v = need[oe]
                self.q[eng].append(lambda E, s=self.sem[oe], vv=v: E.wait_ge(s, vv))
                self.wm[eng][oe] = v
        inc = 16 if eng == "d" else 1
        self.cnt[eng] += inc
        myc = self.cnt[eng]
        self.q[eng].append(lambda E, f=fn, s=self.sem[eng], i=inc: f(E).then_inc(s, i))
        for ap in ins:
            rec = self.buf.setdefault(self._key(ap), {"w": None, "r": {}})
            rec["r"][eng] = myc
        for ap in outs:
            rec = self.buf.setdefault(self._key(ap), {"w": None, "r": {}})
            rec["w"] = (eng, myc)
            rec["r"] = {}

    def t(self, fn, outs, ins):
        self.emit("t", fn, outs, ins)

    def v(self, fn, outs, ins):
        self.emit("v", fn, outs, ins)

    def s(self, fn, outs, ins):
        self.emit("s", fn, outs, ins)

    def dma(self, out, in_):
        out = _ap(out)
        in_ = _ap(in_)
        self.emit("d", lambda E, o=out, i=in_: E.dma_start(o, i), [out], [in_])

    def stage_end(self, name):
        qs = self.q
        self.q = {e: [] for e in "tvsd"}
        blk = self.nc.Block("st%d_%s" % (self.nstage, name))
        self.nstage += 1
        with blk:
            for e, attr in (("d", "sync"), ("t", "tensor"), ("v", "vector"), ("s", "scalar")):
                ops = qs[e]
                if ops:
                    getattr(blk, attr)(lambda E, ops=ops: [f(E) for f in ops])
        self.buf = {}
        for e in "tvsd":
            for o in "tvsd":
                if o != e and self.cnt[o] > self.wm[e][o]:
                    v = self.cnt[o]
                    self.q[e].append(lambda E, s=self.sem[o], vv=v: E.wait_ge(s, vv))
                    self.wm[e][o] = v


def _ap(x):
    if x is None or isinstance(x, (int, float, tuple)):
        return x
    if isinstance(x, bass.AP):
        return x
    return x[:]


def _aps(*xs):
    return [_ap(x) for x in xs]


def MM(k, out, lhsT, rhs, start, stop, tp=None):
    out, lhsT, rhs = _aps(out, lhsT, rhs)
    k.t(lambda E, o=out, l=lhsT, r=rhs, a=start, b=stop, t_=tp:
        E.matmul(o, l, r, start=a, stop=b, tile_position=t_), [out], [lhsT, rhs])


def TR(k, out, in_, ident, tp=None):
    out, in_, ident = _aps(out, in_, ident)
    k.t(lambda E, o=out, i=in_, d=ident, t_=tp: E.matmul(o, i, d, start=True, stop=True,
        is_transpose=True, tile_position=t_), [out], [in_, ident])


def TT(k, out, a, b, op):
    out, a, b = _aps(out, a, b)
    k.v(lambda E, o=out, x=a, y=b, p=op: E.tensor_tensor(o, x, y, p), [out], [a, b])


def TS(k, out, in_, s1, op, s2=None, op2=None):
    out, in_, s1, s2 = _aps(out, in_, s1, s2)
    ins = [in_] + [x for x in (s1, s2) if hasattr(x, "tensor")]
    if s2 is None:
        k.v(lambda E, o=out, i=in_, a=s1, p=op: E.tensor_scalar(o, i, a, None, p),
            [out], ins)
    else:
        k.v(lambda E, o=out, i=in_, a=s1, b=s2, p=op, q=op2:
            E.tensor_scalar(o, i, a, b, p, q), [out], ins)


def STT(k, out, in0, sc, in1, op0, op1):
    out, in0, sc, in1 = _aps(out, in0, sc, in1)
    ins = [in0, in1] + ([sc] if hasattr(sc, "tensor") else [])
    k.v(lambda E, o=out, a=in0, s=sc, b=in1, p=op0, q=op1:
        E.scalar_tensor_tensor(o, a, s, b, p, q), [out], ins)


def CP(k, out, in_):
    out, in_ = _aps(out, in_)
    k.v(lambda E, o=out, i=in_: E.tensor_copy(o, i), [out], [in_])


def RED(k, out, in_, op, negate=False):
    out, in_ = _aps(out, in_)
    k.v(lambda E, o=out, i=in_, p=op, n=negate:
        E.tensor_reduce(o, i, AX, p, negate=n), [out], [in_])


def RECIP(k, out, in_):
    out, in_ = _aps(out, in_)
    k.v(lambda E, o=out, i=in_: E.reciprocal(o, i), [out], [in_])


def ACT(k, out, in_, func, bias=0.0, scale=1.0):
    out, in_, bias, scale = _aps(out, in_, bias, scale)
    ins = [in_] + [x for x in (bias, scale) if hasattr(x, "tensor")]
    k.s(lambda E, o=out, i=in_, f=func, b=bias, sc=scale:
        E.activation(o, i, f, bias=b, scale=sc), [out], ins)


def SMUL(k, out, in_, m):
    out, in_ = _aps(out, in_)
    k.s(lambda E, o=out, i=in_, mm_=m: E.mul(o, i, mm_), [out], [in_])


def MEMSET(k, ap, val=0.0):
    ap = _ap(ap)
    k.v(lambda E, a=ap, v=val: E.memset(a, v), [ap], [])


def build(nc):
    with nc.allow_low_precision(reason="bf16 PE transposes are pure permutations"):
        return _build(nc)


def _build(nc):
    k = K(nc)
    _uid = [0]

    def _sbt(name, shape, dtype):
        _uid[0] += 1
        return nc.sbuf_tensor("%s_u%d" % (name, _uid[0]), shape, dtype)

    def _pst(name, shape, dtype):
        _uid[0] += 1
        return nc.psum_tensor("%s_u%d" % (name, _uid[0]), shape, dtype)
    dram = {}

    def din(name, shape, dtype=F32):
        dram[name] = nc.dram_tensor(name, list(shape), dtype, kind="ExternalInput")
        return dram[name]

    def dint(name, shape, dtype=F32):
        dram[name] = nc.dram_tensor(name, list(shape), dtype, kind="Internal")
        return dram[name]

    din("x_cm", [D, T])
    din("ln1_w", [D]); din("ln1_b", [D])
    din("inpw", [4256, D])
    din("convw", [2176, 4]); din("convb", [2176])
    din("dtb", [32]); din("alog", [32]); din("Dh", [32])
    din("normw", [2048])
    din("outpw", [D, 2048])
    din("ln2_w", [D]); din("ln2_b", [D])
    din("mw1", [4096, D]); din("mb1", [4096])
    din("mw2", [D, 4096]); din("mb2", [D])
    din("fqkv", [6, D, 3072])
    din("fqs", [6, 64]); din("fks", [6, 64])
    din("fwo", [6, D, D])
    din("flg", [6, D]); din("flb", [6, D])
    din("fw1", [6, D, 5460]); din("fb1", [6, 5460])
    din("fw2", [6, 2730, D]); din("fb2", [6, D])
    for nm in ("c_CS", "c_SS", "c_CI", "c_SI"):
        din(nm, [128, T])
    din("c_masks", [128, 2048])
    din("c_ltm", [128, 128]); din("c_ut", [128, 128])
    din("c_if", [128, 128]); din("c_ib", [128, 128], BF)
    din("c_id2f", [128, 64]); din("c_id2b", [128, 64], BF)
    din("c_onesf", [128, 512]); din("c_onesb", [128, 512], BF)
    din("c_sel2b", [128, 2], BF); din("c_sel2tb", [2, 128], BF)
    din("c_rht", [128, 128], BF)
    din("c_scal", [128, 8])
    out_t = nc.dram_tensor("out_cm", [D, T], F32, kind="ExternalOutput")
    dram["out_cm"] = out_t

    dint("xr", [D, T]); dint("u", [D, T]); dint("t0", [D, T]); dint("t1", [D, T])
    dint("zx", [4256, T])
    dint("xbcs", [2176, T], BF)
    dint("dts", [T, 32])
    dint("ymm", [2048, T], BF)
    dint("yn", [2048, T], BF)
    dint("g2d", [2816, T], BF)
    dint("m1d", [4096, T], BF)
    dint("xnb", [D, T], BF)
    dint("tp", [D, T])

    def A(name):
        return dram[name].ap()

    es_glob = ExitStack()

    def sb(name, shape, dtype):
        return es_glob.enter_context(nc.sbuf_tensor(name, list(shape), dtype))

    IF = sb("IF", [128, 128], F32)
    IB = sb("IB", [128, 128], BF)
    ID2F = sb("ID2F", [128, 64], F32)
    ID2B = sb("ID2B", [128, 64], BF)
    ONESF = sb("ONESF", [128, 512], F32)
    ONESB = sb("ONESB", [128, 512], BF)
    SEL2B = sb("SEL2B", [128, 2], BF)
    SEL2TB = sb("SEL2TB", [2, 128], BF)
    RHT = sb("RHT", [128, 128], BF)
    UTF = sb("UTF", [128, 128], F32)
    LTM = sb("LTM", [128, 128], F32)
    MASKS = sb("MASKS", [128, 2048], F32)
    COLS = sb("COLS", [128, 640], F32)
    CSC = sb("CSC", [128, 8], F32)
    _scal_vals = [EPS, 1.0, -1.0, 0.0, 0.5, 8.0, 1e-12, 2.0]
    for _i, _v in enumerate(_scal_vals):
        nc.const_aps.register(F32, _v, CSC[:, _i:_i + 1])

    CL = {}
    _c = [0]

    def cslot(nm, n):
        CL[nm] = _c[0]
        _c[0] += n

    cslot("ln1w", 8); cslot("ln1b", 8); cslot("ln2w", 8); cslot("ln2b", 8)
    cslot("normw", 16); cslot("mb1", 32); cslot("mb2", 8)
    cslot("dtb", 1); cslot("Dh", 1)
    for i in range(6):
        cslot("flg%d" % i, 8); cslot("flb%d" % i, 8)
        cslot("b1a%d" % i, 22); cslot("b1g%d" % i, 22); cslot("fb2_%d" % i, 8)
        cslot("fqs%d" % i, 1); cslot("fks%d" % i, 1)
    assert _c[0] <= 640

    k.dma(CSC[:, 0:8], A("c_scal")[:, :])
    for nm, dst in (("c_if", IF), ("c_ib", IB), ("c_id2f", ID2F), ("c_id2b", ID2B),
                    ("c_onesf", ONESF), ("c_onesb", ONESB), ("c_sel2b", SEL2B),
                    ("c_sel2tb", SEL2TB), ("c_rht", RHT), ("c_ut", UTF),
                    ("c_ltm", LTM), ("c_masks", MASKS)):
        sh = dram[nm].shape
        k.dma(dst[0:sh[0], 0:sh[1]], A(nm)[0:sh[0], 0:sh[1]])

    _es = ExitStack()
    lcstg = _es.enter_context(_sbt("lcstg", [128, 128], F32))
    lcps = _es.enter_context(_pst("lcps", [128, 64], F32))
    if True:

        def load_col(dst_cols, src1d, C, scale=1.0):
            rows = C // 128
            tail = C % 128
            n = rows + (1 if tail else 0)
            if rows:
                k.dma(lcstg[0:rows, 0:128], src1d.rearrange("(r c) -> r c", c=128)
                      if False else src1d[0:rows * 128].rearrange("(r c) -> r c", c=128))
            if tail:
                k.dma(lcstg[rows:rows + 1, 0:tail],
                      src1d[rows * 128:rows * 128 + tail][None, :])
            TR(k, lcps[0:128, 0:n], lcstg[0:n, 0:128], IF[0:n, 0:n])
            if scale == 1.0:
                CP(k, dst_cols[:, 0:n], lcps[0:128, 0:n])
            else:
                SMUL(k, dst_cols[:, 0:n], lcps[0:128, 0:n], scale)

        def colsl(nm, n):
            return COLS[:, CL[nm]:CL[nm] + n]

        load_col(colsl("ln1w", 8), A("ln1_w"), D)
        load_col(colsl("ln1b", 8), A("ln1_b"), D)
        load_col(colsl("ln2w", 8), A("ln2_w"), D)
        load_col(colsl("ln2b", 8), A("ln2_b"), D)
        load_col(colsl("normw", 16), A("normw"), 2048)
        load_col(colsl("mb1", 32), A("mb1"), 4096)
        load_col(colsl("mb2", 8), A("mb2"), D)
        load_col(colsl("dtb", 1), A("dtb"), 32)
        load_col(colsl("Dh", 1), A("Dh"), 32)
        for i in range(6):
            load_col(colsl("flg%d" % i, 8), A("flg")[i], D)
            load_col(colsl("flb%d" % i, 8), A("flb")[i], D)
            load_col(colsl("b1a%d" % i, 22), A("fb1")[i][0:2730], 2730)
            load_col(colsl("b1g%d" % i, 22), A("fb1")[i][2730:5460], 2730)
            load_col(colsl("fb2_%d" % i, 8), A("fb2")[i], D)
            for nm2, src, sc in (("fqs%d" % i, A("fqs")[i], 8.0),
                                 ("fks%d" % i, A("fks")[i], 1.0)):
                k.dma(lcstg[0:1, 0:64], src[0:64][None, :])
                k.dma(lcstg[0:1, 64:128], src[0:64][None, :])
                TR(k, lcps[0:128, 0:1], lcstg[0:1, 0:128], IF[0:1, 0:1])
                if sc == 1.0:
                    CP(k, colsl(nm2, 1), lcps[0:128, 0:1])
                else:
                    SMUL(k, colsl(nm2, 1), lcps[0:128, 0:1], sc)
        k.stage_end("prep")
        _es.close()

    def colsl(nm, n):
        return COLS[:, CL[nm]:CL[nm] + n]

    def ln_pass(name, src, dst, gcol=None, bcol=None):
        _es = ExitStack()
        X = _es.enter_context(_sbt("lnx", [128, 4096], F32))
        SQ = _es.enter_context(_sbt("lnsq", [128, 512], BF))
        XBC = _es.enter_context(_sbt("lnxb", [128, 512], BF))
        ST = _es.enter_context(_sbt("lnst", [1, 2048], F32))
        O = _es.enter_context(_sbt("lnout", [128, 4096], F32))
        P1 = _es.enter_context(_pst("lnp1", [1, 512], F32))
        P2 = _es.enter_context(_pst("lnp2", [1, 512], F32))
        P3 = _es.enter_context(_pst("lnp3", [128, 512], F32))
        P4 = _es.enter_context(_pst("lnp4", [128, 512], F32))
        if True:
            for b in range(NB):
                t0 = b * TB
                for c in range(8):
                    k.dma(X[:, c * 512:(c + 1) * 512], src[c * 128:(c + 1) * 128, t0:t0 + TB])
                for c in range(8):
                    xc = X[:, c * 512:(c + 1) * 512]
                    CP(k, XBC, xc)
                    MM(k, P1, ONESB[:, 0:1], XBC, c == 0, c == 7)
                    ACT(k, SQ, xc, AF.Square)
                    MM(k, P2, ONESB[:, 0:1], SQ, c == 0, c == 7)
                mu = ST[0:1, 0:512]; var = ST[0:1, 512:1024]
                rstd = ST[0:1, 1024:1536]; musq = ST[0:1, 1536:2048]
                TS(k, mu, P1[0:1, :], 1.0 / D, OP.mult)
                TS(k, var, P2[0:1, :], 1.0 / D, OP.mult)
                TT(k, musq, mu, mu, OP.mult)
                TT(k, var, var, musq, OP.subtract)
                ACT(k, rstd, var, AF.Sqrt, bias=EPS)
                RECIP(k, rstd, rstd)
                MM(k, P3, ONESF[0:1, 0:128], mu, True, True)
                MM(k, P4, ONESF[0:1, 0:128], rstd, True, True)
                for c in range(8):
                    xc = X[:, c * 512:(c + 1) * 512]
                    oc = O[:, c * 512:(c + 1) * 512]
                    TT(k, oc, xc, P3, OP.subtract)
                    TT(k, oc, oc, P4, OP.mult)
                    if gcol is not None:
                        TS(k, oc, oc, gcol[:, c:c + 1], OP.mult,
                           bcol[:, c:c + 1], OP.add)
                for c in range(8):
                    k.dma(dst[c * 128:(c + 1) * 128, t0:t0 + TB], O[:, c * 512:(c + 1) * 512])
            k.stage_end(name)
            _es.close()

    def axpy_pass(name, src_x, src_y, dst, alpha):
        _es = ExitStack()
        X = _es.enter_context(_sbt("apx", [128, 4096], F32))
        Y = _es.enter_context(_sbt("apy", [128, 4096], F32))
        O = _es.enter_context(_sbt("apo", [128, 4096], F32))
        if True:
            for b in range(NB):
                t0 = b * TB
                for c in range(8):
                    k.dma(X[:, c * 512:(c + 1) * 512], src_x[c * 128:(c + 1) * 128, t0:t0 + TB])
                    k.dma(Y[:, c * 512:(c + 1) * 512], src_y[c * 128:(c + 1) * 128, t0:t0 + TB])
                for c in range(8):
                    sl = slice(c * 512, (c + 1) * 512)
                    if alpha == 1.0:
                        TT(k, O[:, sl], X[:, sl], Y[:, sl], OP.add)
                    else:
                        STT(k, O[:, sl], Y[:, sl], alpha, X[:, sl], OP.mult, OP.add)
                for c in range(8):
                    k.dma(dst[c * 128:(c + 1) * 128, t0:t0 + TB], O[:, c * 512:(c + 1) * 512])
            k.stage_end(name)
            _es.close()

    def load_w_inout(W0, wap, kchunks, ncols, stg, coloff=0):
        stgw = stg.shape[1] if hasattr(stg, "shape") else 1024
        stgw = 1024
        for c in range(kchunks):
            for p0 in range(0, ncols, stgw):
                pw = min(stgw, ncols - p0)
                k.dma(stg[:, 0:pw], wap[c * 128:(c + 1) * 128, p0:p0 + pw])
                CP(k, W0[:, coloff + c * ncols + p0:coloff + c * ncols + p0 + pw],
                   stg[:, 0:pw])

    def load_w_T(W0, wap, out_rows, in_cols, stg, stgb, coloff=0, row0=0):
        nk = in_cols // 128
        nm = out_rows // 128
        with _pst("jtp", [128, 512], BF) as JP:
            for m in range(nm):
                k.dma(stg[:, 0:in_cols], wap[row0 + m * 128:row0 + (m + 1) * 128, 0:in_cols])
                CP(k, stgb[:, 0:in_cols], stg[:, 0:in_cols])
                for c in range(nk):
                    pslot = c % 4
                    TR(k, JP[:, pslot * 128:(pslot + 1) * 128],
                       stgb[:, c * 128:(c + 1) * 128], IB)
                    CP(k, W0[:, coloff + c * out_rows + m * 128:
                             coloff + c * out_rows + (m + 1) * 128],
                       JP[:, pslot * 128:(pslot + 1) * 128])

    def attn_stage(li, src, dst):
        _es = ExitStack()
        W0 = _es.enter_context(_sbt("aw0", [128, 8 * 3072 + 8 * 1024], BF))
        STG = _es.enter_context(_sbt("astg", [128, 1024], F32))
        XIN = _es.enter_context(_sbt("axin", [128, 4096], F32))
        XU = _es.enter_context(_sbt("axu", [128, 2048], F32))
        XB = _es.enter_context(_sbt("axb", [128, 4096], BF))
        Q = _es.enter_context(_sbt("aq", [128, 4096], F32))
        KK = _es.enter_context(_sbt("ak", [128, 4096], F32))
        QR = _es.enter_context(_sbt("aqr", [128, 4096], BF))
        KW = _es.enter_context(_sbt("akw", [128, 8 * 768], BF))
        VW = _es.enter_context(_sbt("avw", [128, 6 * 1024], BF))
        AO = _es.enter_context(_sbt("aao", [128, 4096], BF))
        ROT = _es.enter_context(_sbt("arot", [128, 2048], F32))
        SM = _es.enter_context(_sbt("asm", [128, 512], F32))
        PM = _es.enter_context(_sbt("apm", [128, 512], F32))
        PB = _es.enter_context(_sbt("apb", [128, 512], BF))
        PT = _es.enter_context(_sbt("apt", [128, 512], BF))
        NRM = _es.enter_context(_sbt("anrm", [2, 512], F32))
        RR = _es.enter_context(_sbt("arr", [2, 512], BF))
        STAT = _es.enter_context(_sbt("astat", [128, 4], F32))
        OF = _es.enter_context(_sbt("aof", [128, 512], F32))
        PS0 = _es.enter_context(_pst("aps0", [128, 512], F32))
        PSIM = _es.enter_context(_pst("apsim", [128, 512], F32))
        PSPT = _es.enter_context(_pst("apspt", [128, 512], BF))
        PSVT = _es.enter_context(_pst("apsvt", [128, 1024], BF))
        PSROT = _es.enter_context(_pst("apsrot", [128, 512], F32))
        PSNN = _es.enter_context(_pst("apsnn", [2, 512], F32))
        PSREP = _es.enter_context(_pst("apsrep", [128, 512], F32))
        PSAV = _es.enter_context(_pst("apsav", [128, 128], F32))
        if True:
            load_w_inout(W0, A("fqkv")[li], 8, 3072, STG)
            load_w_inout(W0, A("fwo")[li], 8, 1024, STG, coloff=8 * 3072)
            WOOFF = 8 * 3072
            qcol = colsl("fqs%d" % li, 1)
            kcol = colsl("fks%d" % li, 1)
            for b in range(NB):
                t0 = b * TB
                for c in range(4):
                    k.dma(XIN[:, c * 512:(c + 1) * 512],
                          src[c * 128:(c + 1) * 128, t0:t0 + TB])
                for c in range(4, 8):
                    if b == 0:
                        MEMSET(k, XIN[:, c * 512:c * 512 + 1], 0.0)
                        k.dma(XIN[:, c * 512 + 1:(c + 1) * 512],
                              src[c * 128:(c + 1) * 128, 0:TB - 1])
                    else:
                        k.dma(XIN[:, c * 512:(c + 1) * 512],
                              src[c * 128:(c + 1) * 128, t0 - 1:t0 + TB - 1])
                for c in range(4):
                    k.dma(XU[:, c * 512:(c + 1) * 512],
                          src[(4 + c) * 128:(5 + c) * 128, t0:t0 + TB])
                for j, nm in enumerate(("c_CS", "c_SS", "c_CI", "c_SI")):
                    k.dma(ROT[:, j * 512:(j + 1) * 512], A(nm)[:, t0:t0 + TB])
                CP(k, XB, XIN)
                if b == 0:
                    MEMSET(k, KW[:, 0:8 * 768], 0.0)
                    MEMSET(k, VW[:, 0:2048], 0.0)
                else:
                    k.v(lambda E, o=KW, i=KW: E.tensor_copy(
                        o.rearrange("p (c w) -> p c w", w=768)[:, :, 0:256],
                        i.rearrange("p (c w) -> p c w", w=768)[:, :, 512:768]),
                        [KW], [KW])
                    CP(k, VW[:, 0:2048], VW[:, 4096:6144])
                for m in range(24):
                    for c in range(8):
                        MM(k, PS0, W0[:, c * 3072 + m * 128:c * 3072 + (m + 1) * 128],
                           XB[:, c * 512:(c + 1) * 512], c == 0, c == 7)
                    if m < 8:
                        CP(k, Q[:, m * 512:(m + 1) * 512], PS0)
                    elif m < 16:
                        CP(k, KK[:, (m - 8) * 512:(m - 7) * 512], PS0)
                    else:
                        CP(k, AO[:, (m - 16) * 512:(m - 15) * 512], PS0)
                for (SRC, ccol) in ((Q, qcol), (KK, kcol)):
                    for c in range(8):
                        xc = SRC[:, c * 512:(c + 1) * 512]
                        ACT(k, PB, xc, AF.Square)
                        MM(k, PSNN, SEL2B, PB, True, True)
                        ACT(k, NRM, PSNN, AF.Sqrt)
                        TS(k, NRM, NRM, 1e-12, OP.max)
                        RECIP(k, RR, NRM)
                        MM(k, PSREP, SEL2TB, RR, True, True)
                        STT(k, xc, xc, ccol, PSREP, OP.mult, OP.mult)
                for (SRC, COSO, SINO, DST) in ((Q, 0, 512, None), (KK, 1024, 1536, None)):
                    for c in range(8):
                        xc = SRC[:, c * 512:(c + 1) * 512]
                        CP(k, PB, xc)
                        MM(k, PSROT, RHT, PB, True, True)
                        TT(k, SM, xc, ROT[:, COSO:COSO + 512], OP.mult)
                        TT(k, PM, PSROT, ROT[:, SINO:SINO + 512], OP.mult)
                        if SRC is Q:
                            TT(k, QR[:, c * 512:(c + 1) * 512], SM, PM, OP.add)
                        else:
                            TT(k, KW[:, c * 768 + 256:c * 768 + 768], SM, PM, OP.add)
                for ti in range(4):
                    for m in range(8):
                        TR(k, PSVT[:, m * 128:(m + 1) * 128],
                           AO[:, m * 512 + ti * 128:m * 512 + (ti + 1) * 128], IB)
                    CP(k, VW[:, (2 + ti) * 1024:(3 + ti) * 1024], PSVT)
                for wl in range(2):
                    wg = 2 * b + wl
                    for h in range(16):
                        c = h // 2
                        pb_ = 64 * (h % 2)
                        for ic in range(2):
                            qsl = QR[pb_:pb_ + 64,
                                     c * 512 + wl * 256 + ic * 128:
                                     c * 512 + wl * 256 + (ic + 1) * 128]
                            ksl = KW[pb_:pb_ + 64, c * 768 + wl * 256:c * 768 + wl * 256 + 512]
                            MM(k, PSIM, qsl, ksl, True, True)
                            moff = ((2 if wg == 0 else 0) + ic) * 512
                            TT(k, SM, PSIM, MASKS[:, moff:moff + 512], OP.add)
                            RED(k, STAT[:, 0:1], SM, OP.max, negate=True)
                            ACT(k, PM, SM, AF.Exp, bias=STAT[:, 0:1])
                            RED(k, STAT[:, 1:2], PM, OP.add)
                            RECIP(k, STAT[:, 2:3], STAT[:, 1:2])
                            TS(k, PB, PM, STAT[:, 2:3], OP.mult)
                            for jc in range(4):
                                TR(k, PSPT[:, jc * 128:(jc + 1) * 128],
                                   PB[:, jc * 128:(jc + 1) * 128], IB)
                            CP(k, PT, PSPT)
                            tp = None if pb_ == 0 else (0, 64)
                            for jc in range(4):
                                rt_ = wl * 2 + jc
                                MM(k, PSAV[pb_:pb_ + 64, :],
                                   VW[:, rt_ * 1024 + h * 64:rt_ * 1024 + (h + 1) * 64],
                                   PT[:, jc * 128:(jc + 1) * 128], jc == 0, jc == 3,
                                   tp=tp)
                            CP(k, AO[pb_:pb_ + 64,
                                     c * 512 + wl * 256 + ic * 128:
                                     c * 512 + wl * 256 + (ic + 1) * 128],
                               PSAV[pb_:pb_ + 64, :])
                for m in range(8):
                    for c in range(8):
                        MM(k, PS0, W0[:, WOOFF + c * 1024 + m * 128:
                                      WOOFF + c * 1024 + (m + 1) * 128],
                           AO[:, c * 512:(c + 1) * 512], c == 0, c == 7)
                    res = XIN[:, m * 512:(m + 1) * 512] if m < 4 \
                        else XU[:, (m - 4) * 512:(m - 3) * 512]
                    TT(k, OF, PS0, res, OP.add)
                    k.dma(dst[m * 128:(m + 1) * 128, t0:t0 + TB], OF)
            k.stage_end("attn%d" % li)
            _es.close()

    def ffn1_stage(li, src, half):
        pairs = list(range(0, 11)) if half == 0 else list(range(11, 22))
        _es = ExitStack()
        W0 = _es.enter_context(_sbt("fw0s", [128, 8 * 2816], BF))
        STG = _es.enter_context(_sbt("fstg", [128, 2816], F32))
        XIN = _es.enter_context(_sbt("fxin", [128, 4096], F32))
        XB = _es.enter_context(_sbt("fxb", [128, 4096], BF))
        SQ = _es.enter_context(_sbt("fsq", [128, 512], BF))
        ST = _es.enter_context(_sbt("fst", [1, 2048], F32))
        AFB = _es.enter_context(_sbt("faf", [128, 512], F32))
        GG = _es.enter_context(_sbt("fgg", [128, 512], F32))
        G2 = _es.enter_context(_sbt("fg2", [128, 512], BF))
        P1 = _es.enter_context(_pst("fp1", [1, 512], F32))
        P2 = _es.enter_context(_pst("fp2", [1, 512], F32))
        P3 = _es.enter_context(_pst("fp3", [128, 512], F32))
        P4 = _es.enter_context(_pst("fp4", [128, 512], F32))
        PA = _es.enter_context(_pst("fpa", [128, 512], F32))
        PG = _es.enter_context(_pst("fpg", [128, 512], F32))
        if True:
            for c in range(8):
                wap = A("fw1")[li]
                for idx, p in enumerate(pairs):
                    w = min(128, 2730 - p * 128)
                    k.dma(STG[:, idx * 128:idx * 128 + w],
                          wap[c * 128:(c + 1) * 128, p * 128:p * 128 + w])
                    k.dma(STG[:, 1408 + idx * 128:1408 + idx * 128 + w],
                          wap[c * 128:(c + 1) * 128, 2730 + p * 128:2730 + p * 128 + w])
                CP(k, W0[:, c * 2816:(c + 1) * 2816], STG[:, 0:2816])
            gcol = colsl("flg%d" % li, 8)
            bcol = colsl("flb%d" % li, 8)
            b1a = colsl("b1a%d" % li, 22)
            b1g = colsl("b1g%d" % li, 22)
            for b in range(NB):
                t0 = b * TB
                if half == 0:
                    for c in range(4):
                        k.dma(XIN[:, c * 512:(c + 1) * 512],
                              src[c * 128:(c + 1) * 128, t0:t0 + TB])
                    for c in range(4, 8):
                        if b == 0:
                            MEMSET(k, XIN[:, c * 512:c * 512 + 1], 0.0)
                            k.dma(XIN[:, c * 512 + 1:(c + 1) * 512],
                                  src[c * 128:(c + 1) * 128, 0:TB - 1])
                        else:
                            k.dma(XIN[:, c * 512:(c + 1) * 512],
                                  src[c * 128:(c + 1) * 128, t0 - 1:t0 + TB - 1])
                    for c in range(8):
                        xc = XIN[:, c * 512:(c + 1) * 512]
                        CP(k, SQ, xc)
                        MM(k, P1, ONESB[:, 0:1], SQ, c == 0, c == 7)
                        ACT(k, SQ, xc, AF.Square)
                        MM(k, P2, ONESB[:, 0:1], SQ, c == 0, c == 7)
                    mu = ST[0:1, 0:512]; var = ST[0:1, 512:1024]
                    rstd = ST[0:1, 1024:1536]; musq = ST[0:1, 1536:2048]
                    TS(k, mu, P1[0:1, :], 1.0 / D, OP.mult)
                    TS(k, var, P2[0:1, :], 1.0 / D, OP.mult)
                    TT(k, musq, mu, mu, OP.mult)
                    TT(k, var, var, musq, OP.subtract)
                    ACT(k, rstd, var, AF.Sqrt, bias=EPS)
                    RECIP(k, rstd, rstd)
                    MM(k, P3, ONESF[0:1, 0:128], mu, True, True)
                    MM(k, P4, ONESF[0:1, 0:128], rstd, True, True)
                    for c in range(8):
                        xc = XIN[:, c * 512:(c + 1) * 512]
                        TT(k, xc, xc, P3, OP.subtract)
                        TT(k, xc, xc, P4, OP.mult)
                        TS(k, XB[:, c * 512:(c + 1) * 512], xc,
                           gcol[:, c:c + 1], OP.mult, bcol[:, c:c + 1], OP.add)
                        k.dma(A("xnb")[c * 128:(c + 1) * 128, t0:t0 + TB],
                              XB[:, c * 512:(c + 1) * 512])
                else:
                    for c in range(8):
                        k.dma(XB[:, c * 512:(c + 1) * 512],
                              A("xnb")[c * 128:(c + 1) * 128, t0:t0 + TB])
                for idx, p in enumerate(pairs):
                    w = min(128, 2730 - p * 128)
                    for c in range(8):
                        MM(k, PA[0:w, :], W0[:, c * 2816 + idx * 128:c * 2816 + idx * 128 + w],
                           XB[:, c * 512:(c + 1) * 512], c == 0, c == 7)
                    for c in range(8):
                        MM(k, PG[0:w, :],
                           W0[:, c * 2816 + 1408 + idx * 128:c * 2816 + 1408 + idx * 128 + w],
                           XB[:, c * 512:(c + 1) * 512], c == 0, c == 7)
                    TS(k, AFB[0:w, :], PA[0:w, :], b1a[0:w, p:p + 1], OP.add)
                    ACT(k, GG[0:w, :], PG[0:w, :], AF.Gelu, bias=b1g[0:w, p:p + 1])
                    TT(k, G2[0:w, :], AFB[0:w, :], GG[0:w, :], OP.mult)
                    k.dma(A("g2d")[p * 128:p * 128 + w, t0:t0 + TB], G2[0:w, :])
            k.stage_end("ffn1_%d_%d" % (li, half))
            _es.close()

    def ffn2_stage(li, res, dst):
        _es = ExitStack()
        W0 = _es.enter_context(_sbt("f2w0", [128, 22 * 1024], BF))
        STG = _es.enter_context(_sbt("f2stg", [128, 1024], F32))
        G2 = _es.enter_context(_sbt("f2g", [128, 22 * 512], BF))
        RS = _es.enter_context(_sbt("f2r", [128, 4096], F32))
        O = _es.enter_context(_sbt("f2o", [128, 512], F32))
        PS = _es.enter_context(_pst("f2p", [128, 512], F32))
        if True:
            wap = A("fw2")[li]
            for c in range(22):
                w = min(128, 2730 - c * 128)
                k.dma(STG[0:w, 0:1024], wap[c * 128:c * 128 + w, 0:1024])
                CP(k, W0[0:w, c * 1024:(c + 1) * 1024], STG[0:w, 0:1024])
            b2 = colsl("fb2_%d" % li, 8)
            for b in range(NB):
                t0 = b * TB
                for c in range(22):
                    w = min(128, 2730 - c * 128)
                    k.dma(G2[0:w, c * 512:(c + 1) * 512],
                          A("g2d")[c * 128:c * 128 + w, t0:t0 + TB])
                for c in range(8):
                    k.dma(RS[:, c * 512:(c + 1) * 512], res[c * 128:(c + 1) * 128, t0:t0 + TB])
                for m in range(8):
                    for c in range(22):
                        w = min(128, 2730 - c * 128)
                        MM(k, PS, W0[0:w, c * 1024 + m * 128:c * 1024 + (m + 1) * 128],
                           G2[0:w, c * 512:(c + 1) * 512], c == 0, c == 21)
                    STT(k, O, PS, b2[:, m:m + 1], RS[:, m * 512:(m + 1) * 512],
                        OP.add, OP.add)
                    k.dma(dst[m * 128:(m + 1) * 128, t0:t0 + TB], O)
            k.stage_end("ffn2_%d" % li)
            _es.close()

    def inproj_stage(half):
        r0 = 0 if half == 0 else 2176
        nm_ = 17 if half == 0 else 16
        rows = 2176 if half == 0 else 2080
        nmch = (rows + 127) // 128
        _es = ExitStack()
        W0 = _es.enter_context(_sbt("ipw0", [128, 8 * 2176], BF))
        STG = _es.enter_context(_sbt("ipstg", [128, 1024], F32))
        STGB = _es.enter_context(_sbt("ipstgb", [128, 1024], BF))
        XIN = _es.enter_context(_sbt("ipx", [128, 4096], F32))
        XB = _es.enter_context(_sbt("ipxb", [128, 4096], BF))
        O = _es.enter_context(_sbt("ipo", [128, 512], F32))
        PS = _es.enter_context(_pst("ipp", [128, 512], F32))
        if True:
            with _pst("ipjt", [128, 512], BF) as JP:
                for m in range(nmch):
                    w = min(128, rows - m * 128)
                    k.dma(STG[0:w, 0:1024], A("inpw")[r0 + m * 128:r0 + m * 128 + w, 0:1024])
                    CP(k, STGB[0:w, 0:1024], STG[0:w, 0:1024])
                    for c in range(8):
                        pslot = c % 4
                        TR(k, JP[:, pslot * 128:pslot * 128 + w],
                           STGB[0:w, c * 128:(c + 1) * 128], IB[0:w, 0:w])
                        CP(k, W0[:, c * rows + m * 128:c * rows + m * 128 + w],
                           JP[:, pslot * 128:pslot * 128 + w])
                for b in range(NB):
                    t0 = b * TB
                    for c in range(8):
                        k.dma(XIN[:, c * 512:(c + 1) * 512],
                              A("t1")[c * 128:(c + 1) * 128, t0:t0 + TB])
                    CP(k, XB, XIN)
                    for m in range(nmch):
                        w = min(128, rows - m * 128)
                        for c in range(8):
                            MM(k, PS[0:w, :], W0[:, c * rows + m * 128:c * rows + m * 128 + w],
                               XB[:, c * 512:(c + 1) * 512], c == 0, c == 7)
                        CP(k, O[0:w, :], PS[0:w, :])
                        k.dma(A("zx")[r0 + m * 128:r0 + m * 128 + w, t0:t0 + TB], O[0:w, :])
            k.stage_end("inproj%d" % half)
            _es.close()

    def convdt_stage():
        HB = 256
        _es = ExitStack()
        CW = _es.enter_context(_sbt("cvw", [128, 17 * 4], F32))
        CB = _es.enter_context(_sbt("cvb", [128, 17], F32))
        CSTG = _es.enter_context(_sbt("cvstg", [17, 512], F32))
        XIN = _es.enter_context(_sbt("cvx", [128, 17 * 259], F32))
        ACC = _es.enter_context(_sbt("cvacc", [128, 256], F32))
        SIL = _es.enter_context(_sbt("cvsil", [128, 17 * 256], BF))
        DTI = _es.enter_context(_sbt("cvdt", [32, 256], F32))
        DW = _es.enter_context(_sbt("cvdw", [32, 256], F32))
        DW2 = _es.enter_context(_sbt("cvd2", [32, 256], F32))
        CVP = _es.enter_context(_pst("cvp", [128, 64], F32))
        CVPT = _es.enter_context(_pst("cvpt", [128, 64], F32))
        if True:
            k.dma(CSTG[0:17, 0:512], A("convw").rearrange("(r p) k -> r (p k)", p=128))
            for tap in range(4):
                TR(k, CVP[:, 0:17],
                   CSTG.rearrange("r (p k) -> r k p", k=4)[0:17, tap], IF[0:17, 0:17])
                CP(k, CW[:, tap * 17:(tap + 1) * 17], CVP[:, 0:17])
            k.dma(CSTG[0:17, 0:128], A("convb").rearrange("(r c) -> r c", c=128))
            TR(k, CVP[:, 0:17], CSTG[0:17, 0:128], IF[0:17, 0:17])
            CP(k, CB, CVP[:, 0:17])
            dtbc = colsl("dtb", 1)
            for hb in range(T // HB):
                t0 = hb * HB
                for c in range(17):
                    if hb == 0:
                        MEMSET(k, XIN[:, c * 259:c * 259 + 3], 0.0)
                        k.dma(XIN[:, c * 259 + 3:c * 259 + 259],
                              A("zx")[2048 + c * 128:2048 + (c + 1) * 128, 0:HB])
                    else:
                        k.dma(XIN[:, c * 259:c * 259 + 259],
                              A("zx")[2048 + c * 128:2048 + (c + 1) * 128, t0 - 3:t0 + HB])
                for c in range(17):
                    x0 = XIN[:, c * 259:c * 259 + 256]
                    TS(k, ACC, x0, CW[:, 0 * 17 + c:0 * 17 + c + 1], OP.mult,
                       CB[:, c:c + 1], OP.add)
                    for tap in range(1, 4):
                        xk = XIN[:, c * 259 + tap:c * 259 + tap + 256]
                        STT(k, ACC, xk, CW[:, tap * 17 + c:tap * 17 + c + 1], ACC,
                            OP.mult, OP.add)
                    ACT(k, SIL[:, c * 256:(c + 1) * 256], ACC, AF.Silu)
                    k.dma(A("xbcs")[c * 128:(c + 1) * 128, t0:t0 + HB],
                          SIL[:, c * 256:(c + 1) * 256])
                k.dma(DTI, A("zx")[4224:4256, t0:t0 + HB])
                TS(k, DTI, DTI, dtbc[0:32, :], OP.add)
                TS(k, DW2, DTI, -1.0, OP.mult)
                TT(k, DW, DTI, DW2, OP.max)
                ACT(k, DW, DW, AF.Exp, scale=-1.0)
                ACT(k, DW, DW, AF.Ln, bias=1.0)
                ACT(k, DW2, DTI, AF.Relu)
                TT(k, DW, DW, DW2, OP.add)
                for half in range(2):
                    TR(k, CVPT[:, 0:32], DW[0:32, half * 128:(half + 1) * 128],
                       IF[0:32, 0:32])
                    CP(k, ACC[:, 0:32], CVPT[:, 0:32])
                    k.dma(A("dts")[t0 + half * 128:t0 + (half + 1) * 128, 0:32],
                          ACC[:, 0:32])
            k.stage_end("convdt")
            _es.close()

    def scan_stage2():
        _es = ExitStack()
        XS = _es.enter_context(_sbt("scx", [128, 17 * 512], BF))
        C0 = _es.enter_context(_sbt("scc0", [128, 128], BF))
        DTT = _es.enter_context(_sbt("scdt", [128, 4 * 32], F32))
        S = _es.enter_context(_sbt("scS", [128, 32], F32))
        ES = _es.enter_context(_sbt("sceS", [128, 32], F32))
        AREP = _es.enter_context(_sbt("scA", [128, 32], F32))
        ATOK = _es.enter_context(_sbt("scar", [128, 32], F32))
        DREP = _es.enter_context(_sbt("scd64", [128, 32], F32))
        GS = _es.enter_context(_sbt("scg", [128, 128], F32))
        BT = _es.enter_context(_sbt("scbt", [128, 64], BF))
        VE = _es.enter_context(_sbt("scv", [128, 128], F32))
        VEX = _es.enter_context(_sbt("scve", [128, 128], F32))
        GW = _es.enter_context(_sbt("scgw", [128, 128], BF))
        SCT = _es.enter_context(_sbt("scsc", [128, 128], F32))
        ECT = _es.enter_context(_sbt("scec", [128, 64], F32))
        XP = _es.enter_context(_sbt("scxp", [128, 64], BF))
        XPP = _es.enter_context(_sbt("scxpp", [128, 64], BF))
        CT = _es.enter_context(_sbt("scct", [128, 128], BF))
        DA = _es.enter_context(_sbt("scda", [128, 2], F32))
        H32 = _es.enter_context(_sbt("sch32", [64, 4096], F32))
        HBF = _es.enter_context(_sbt("schbf", [64, 2048], BF))
        Y = _es.enter_context(_sbt("scy", [128, 16 * 512], BF))
        ROW1 = _es.enter_context(_sbt("scrow", [1, 32], F32))
        PS_S = _es.enter_context(_pst("scps_s", [128, 32], F32))
        PS_ROW = _es.enter_context(_pst("scps_row", [128, 128], F32))
        PS_XT = _es.enter_context(_pst("scps_xt", [128, 64], BF))
        PS_Y = _es.enter_context(_pst("scps_y", [128, 128], F32))
        PS_EC = _es.enter_context(_pst("scps_ec", [128, 128], F32))
        PS_HD = _es.enter_context(_pst("scps_hd", [128, 64], F32))
        PS_G = _es.enter_context(_pst("scps_g", [128, 128], F32))
        PS_BT = _es.enter_context(_pst("scps_bt", [128, 128], BF))
        if True:
            k.dma(ROW1, A("alog")[0:32][None, :])
            ACT(k, ROW1, ROW1, AF.Exp)
            SMUL(k, ROW1, ROW1, -1.0)
            MM(k, PS_S, ONESF[0:1, 0:128], ROW1, True, True)
            CP(k, AREP, PS_S)
            k.dma(ROW1, A("Dh")[0:32][None, :])
            MM(k, PS_S, ONESF[0:1, 0:128], ROW1, True, True)
            CP(k, DREP, PS_S)
            MEMSET(k, H32[0:64, 0:4096], 0.0)
            MEMSET(k, HBF[0:64, 0:2048], 0.0)
            for b in range(NB):
                t0 = b * TB
                for c in range(17):
                    k.dma(XS[:, c * 512:(c + 1) * 512],
                          A("xbcs")[c * 128:(c + 1) * 128, t0:t0 + TB])
                for cc in range(4):
                    k.dma(DTT[:, cc * 32:(cc + 1) * 32],
                          A("dts")[t0 + cc * 128:t0 + (cc + 1) * 128, 0:32])
                for cc in range(4):
                    ch = b * 4 + cc
                    pg = ch % 2
                    dtc = DTT[:, cc * 32:(cc + 1) * 32]
                    TT(k, ATOK, dtc, AREP, OP.mult)
                    MM(k, PS_S, UTF, ATOK, True, True)
                    CP(k, S, PS_S)
                    ACT(k, ES, S, AF.Exp)
                    Bcm = XS[0:64, 16 * 512 + cc * 128:16 * 512 + (cc + 1) * 128]
                    Ccm = XS[64:128, 16 * 512 + cc * 128:16 * 512 + (cc + 1) * 128]
                    TR(k, PS_XT, Ccm, ID2B[64:128, 0:64])
                    CP(k, XP, PS_XT)
                    TR(k, PS_BT[0:64, 0:128], XP, IB)
                    CP(k, C0[0:64, :], PS_BT[0:64, 0:128])
                    MM(k, PS_G, Bcm, C0[0:64, :], True, True)
                    CP(k, GS, PS_G)
                    TR(k, PS_BT[:, 0:64], Bcm, ID2B[0:64, 0:64])
                    CP(k, BT, PS_BT[:, 0:64])
                    for h in range(32):
                        pb_ = 64 * (h % 2)
                        hsl = slice(pb_, pb_ + 64)
                        tp = None if pb_ == 0 else (0, 64)
                        Xcm = XS[hsl, (h // 2) * 512 + cc * 128:(h // 2) * 512 + (cc + 1) * 128]
                        hcol = h * 64
                        hold = H32[0:64, pg * 2048 + hcol:pg * 2048 + hcol + 64]
                        hnew = H32[0:64, (1 - pg) * 2048 + hcol:(1 - pg) * 2048 + hcol + 64]
                        hbf = HBF[0:64, hcol:hcol + 64]
                        TS(k, SCT, ONESF[:, 0:128], S[:, h:h + 1], OP.mult)
                        TR(k, PS_ROW, SCT, IF)
                        STT(k, VE, PS_ROW, S[:, h:h + 1], LTM, OP.subtract, OP.add)
                        ACT(k, VEX, VE, AF.Exp)
                        TT(k, GW, VEX, GS, OP.mult)
                        TR(k, PS_XT, Xcm, ID2B[hsl, 0:64])
                        TS(k, XP, PS_XT, dtc[:, h:h + 1], OP.mult)
                        TS(k, XPP, XP, VEX[:, 127:128], OP.mult)
                        MM(k, PS_Y[hsl, :], XP, GW, True, False, tp=tp)
                        TS(k, ECT, ONESF[:, 0:64], ES[:, h:h + 1], OP.mult)
                        TR(k, PS_EC[0:64, :], ECT, IF)
                        TT(k, CT[0:64, :], C0[0:64, :], PS_EC[0:64, :], OP.mult)
                        CP(k, DA[0:64, 0:1], PS_EC[0:64, 127:128])
                        MM(k, PS_Y[hsl, :], hbf, CT[0:64, :], False, True, tp=tp)
                        ysl = Y[hsl, (h // 2) * 512 + cc * 128:(h // 2) * 512 + (cc + 1) * 128]
                        STT(k, ysl, Xcm, DREP[hsl, h:h + 1], PS_Y[hsl, :], OP.mult, OP.add)
                        MM(k, PS_HD[0:64, 0:64], BT, XPP, True, True)
                        STT(k, hnew, hold, DA[0:64, 0:1], PS_HD[0:64, 0:64], OP.mult, OP.add)
                        CP(k, hbf, hnew)
                for c in range(16):
                    k.dma(A("ymm")[c * 128:(c + 1) * 128, t0:t0 + TB],
                          Y[:, c * 512:(c + 1) * 512])
            k.stage_end("scan")
            _es.close()

    def gate_stage():
        HB = 256
        _es = ExitStack()
        YB = _es.enter_context(_sbt("gty", [128, 16 * 256], BF))
        Z = _es.enter_context(_sbt("gtz", [128, 16 * 256], F32))
        YG = _es.enter_context(_sbt("gtyg", [128, 16 * 256], F32))
        SQ = _es.enter_context(_sbt("gtsq", [128, 256], BF))
        ST = _es.enter_context(_sbt("gtst", [1, 256], F32))
        RB = _es.enter_context(_sbt("gtrb", [1, 256], BF))
        O = _es.enter_context(_sbt("gto", [128, 256], BF))
        P1 = _es.enter_context(_pst("gtp1", [1, 256], F32))
        P2 = _es.enter_context(_pst("gtp2", [128, 256], F32))
        if True:
            nw = colsl("normw", 16)
            for hb in range(T // HB):
                t0 = hb * HB
                for c in range(16):
                    k.dma(YB[:, c * 256:(c + 1) * 256],
                          A("ymm")[c * 128:(c + 1) * 128, t0:t0 + HB])
                    k.dma(Z[:, c * 256:(c + 1) * 256],
                          A("zx")[c * 128:(c + 1) * 128, t0:t0 + HB])
                for c in range(16):
                    zc = Z[:, c * 256:(c + 1) * 256]
                    ACT(k, zc, zc, AF.Silu)
                    TT(k, YG[:, c * 256:(c + 1) * 256], YB[:, c * 256:(c + 1) * 256],
                       zc, OP.mult)
                for c in range(16):
                    ACT(k, SQ, YG[:, c * 256:(c + 1) * 256], AF.Square)
                    MM(k, P1, ONESB[:, 0:1], SQ, c == 0, c == 15)
                TS(k, ST, P1[0:1, :], 1.0 / 2048.0, OP.mult)
                ACT(k, ST, ST, AF.Sqrt, bias=EPS)
                RECIP(k, RB, ST)
                MM(k, P2, ONESB[0:1, 0:128], RB, True, True)
                for c in range(16):
                    STT(k, O, YG[:, c * 256:(c + 1) * 256], nw[:, c:c + 1], P2,
                        OP.mult, OP.mult)
                    k.dma(A("yn")[c * 128:(c + 1) * 128, t0:t0 + HB], O)
            k.stage_end("gate")
            _es.close()

    def outproj_stage():
        _es = ExitStack()
        W0 = _es.enter_context(_sbt("opw0", [128, 16 * 1024], BF))
        STG = _es.enter_context(_sbt("opstg", [128, 2048], F32))
        STGB = _es.enter_context(_sbt("opstgb", [128, 2048], BF))
        YN = _es.enter_context(_sbt("opy", [128, 16 * 512], BF))
        RS = _es.enter_context(_sbt("opr", [128, 4096], F32))
        O = _es.enter_context(_sbt("opo", [128, 512], F32))
        PS = _es.enter_context(_pst("opp", [128, 512], F32))
        if True:
            load_w_T(W0, A("outpw"), 1024, 2048, STG, STGB)
            for b in range(NB):
                t0 = b * TB
                for c in range(16):
                    k.dma(YN[:, c * 512:(c + 1) * 512],
                          A("yn")[c * 128:(c + 1) * 128, t0:t0 + TB])
                for c in range(8):
                    k.dma(RS[:, c * 512:(c + 1) * 512],
                          A("t0")[c * 128:(c + 1) * 128, t0:t0 + TB])
                for m in range(8):
                    for c in range(16):
                        MM(k, PS, W0[:, c * 1024 + m * 128:c * 1024 + (m + 1) * 128],
                           YN[:, c * 512:(c + 1) * 512], c == 0, c == 15)
                    TT(k, O, PS, RS[:, m * 512:(m + 1) * 512], OP.add)
                    k.dma(A("t0")[m * 128:(m + 1) * 128, t0:t0 + TB], O)
            k.stage_end("outproj")
            _es.close()

    def mlp1_stage(half):
        r0 = half * 2048
        _es = ExitStack()
        W0 = _es.enter_context(_sbt("m1w0", [128, 8 * 2048], BF))
        STG = _es.enter_context(_sbt("m1stg", [128, 1024], F32))
        STGB = _es.enter_context(_sbt("m1stgb", [128, 1024], BF))
        XIN = _es.enter_context(_sbt("m1x", [128, 4096], F32))
        XB = _es.enter_context(_sbt("m1xb", [128, 4096], BF))
        O = _es.enter_context(_sbt("m1o", [128, 512], BF))
        PS = _es.enter_context(_pst("m1p", [128, 512], F32))
        if True:
            with _pst("m1jt", [128, 512], BF) as JP:
                for m in range(16):
                    k.dma(STG[:, 0:1024], A("mw1")[r0 + m * 128:r0 + (m + 1) * 128, 0:1024])
                    CP(k, STGB[:, 0:1024], STG[:, 0:1024])
                    for c in range(8):
                        pslot = c % 4
                        TR(k, JP[:, pslot * 128:(pslot + 1) * 128],
                           STGB[:, c * 128:(c + 1) * 128], IB)
                        CP(k, W0[:, c * 2048 + m * 128:c * 2048 + (m + 1) * 128],
                           JP[:, pslot * 128:(pslot + 1) * 128])
                mb1 = colsl("mb1", 32)
                for b in range(NB):
                    t0 = b * TB
                    for c in range(8):
                        k.dma(XIN[:, c * 512:(c + 1) * 512],
                              A("t1")[c * 128:(c + 1) * 128, t0:t0 + TB])
                    CP(k, XB, XIN)
                    for m in range(16):
                        for c in range(8):
                            MM(k, PS, W0[:, c * 2048 + m * 128:c * 2048 + (m + 1) * 128],
                               XB[:, c * 512:(c + 1) * 512], c == 0, c == 7)
                        ACT(k, O, PS, AF.Gelu, bias=mb1[:, half * 16 + m:half * 16 + m + 1])
                        k.dma(A("m1d")[r0 + m * 128:r0 + (m + 1) * 128, t0:t0 + TB], O)
            k.stage_end("mlp1_%d" % half)
            _es.close()

    def mlp2_stage(half):
        r0 = half * 2048
        _es = ExitStack()
        W0 = _es.enter_context(_sbt("m2w0", [128, 16 * 1024], BF))
        STG = _es.enter_context(_sbt("m2stg", [128, 2048], F32))
        STGB = _es.enter_context(_sbt("m2stgb", [128, 2048], BF))
        XB = _es.enter_context(_sbt("m2x", [128, 16 * 512], BF))
        RS = _es.enter_context(_sbt("m2r", [128, 4096], F32))
        O = _es.enter_context(_sbt("m2o", [128, 512], F32))
        PS = _es.enter_context(_pst("m2p", [128, 512], F32))
        if True:
            with _pst("m2jt", [128, 512], BF) as JP:
                for m in range(8):
                    k.dma(STG[:, 0:2048],
                          A("mw2")[m * 128:(m + 1) * 128, r0:r0 + 2048])
                    CP(k, STGB[:, 0:2048], STG[:, 0:2048])
                    for c in range(16):
                        pslot = c % 4
                        TR(k, JP[:, pslot * 128:(pslot + 1) * 128],
                           STGB[:, c * 128:(c + 1) * 128], IB)
                        CP(k, W0[:, c * 1024 + m * 128:c * 1024 + (m + 1) * 128],
                           JP[:, pslot * 128:(pslot + 1) * 128])
                mb2 = colsl("mb2", 8)
                for b in range(NB):
                    t0 = b * TB
                    for c in range(16):
                        k.dma(XB[:, c * 512:(c + 1) * 512],
                              A("m1d")[r0 + c * 128:r0 + (c + 1) * 128, t0:t0 + TB])
                    if half == 1:
                        for c in range(8):
                            k.dma(RS[:, c * 512:(c + 1) * 512],
                                  A("tp")[c * 128:(c + 1) * 128, t0:t0 + TB])
                    for m in range(8):
                        for c in range(16):
                            MM(k, PS, W0[:, c * 1024 + m * 128:c * 1024 + (m + 1) * 128],
                               XB[:, c * 512:(c + 1) * 512], c == 0, c == 15)
                        if half == 0:
                            CP(k, O, PS)
                        else:
                            STT(k, O, PS, mb2[:, m:m + 1],
                                RS[:, m * 512:(m + 1) * 512], OP.add, OP.add)
                        k.dma(A("tp")[m * 128:(m + 1) * 128, t0:t0 + TB], O)
            k.stage_end("mlp2_%d" % half)
            _es.close()

    def mlp2b_addt0():
        axpy_pass("mlpres", A("t0"), A("tp"), A("t0"), 1.0)

    def ffn_module(src_res):
        cur = "u"
        oth = "t0"
        for i in range(6):
            attn_stage(i, A(cur), A(oth))
            ffn1_stage(i, A(oth), 0)
            ffn1_stage(i, A(oth), 1)
            ffn2_stage(i, A(oth), A(cur))
        return cur

    ln_pass("lnA", A("x_cm"), A("u"))
    fin = ffn_module(None)
    axpy_pass("resC", A("x_cm"), A(fin), A("xr"), 0.5)
    ln_pass("lnD", A("xr"), A("t0"))
    ln_pass("lnE", A("t0"), A("t1"), colsl("ln1w", 8), colsl("ln1b", 8))
    inproj_stage(0)
    inproj_stage(1)
    convdt_stage()
    scan_stage2()
    gate_stage()
    outproj_stage()
    ln_pass("lnF6", A("t0"), A("t1"), colsl("ln2w", 8), colsl("ln2b", 8))
    mlp1_stage(0)
    mlp1_stage(1)
    mlp2_stage(0)
    mlp2_stage(1)
    mlp2b_addt0()
    axpy_pass("resF9", A("xr"), A("t0"), A("xr"), 1.0)
    ln_pass("lnG", A("xr"), A("u"))
    fin = ffn_module(None)
    axpy_pass("resI", A("xr"), A(fin), A("out_cm"), 0.5)

    blk = nc.Block("fin")
    with blk:
        cnts = dict(k.cnt)
        sems = dict(k.sem)
        qd = k.q["d"]

        def _fin(E, qd=qd, cnts=cnts, sems=sems):
            for f in qd:
                f(E)
            for o in "tvs":
                if cnts[o]:
                    E.wait_ge(sems[o], cnts[o])
            E.wait_ge(sems["d"], cnts["d"])
        blk.sync(_fin)
        for e, attr in (("t", "tensor"), ("v", "vector"), ("s", "scalar")):
            ops = k.q[e]
            if ops:
                getattr(blk, attr)(lambda E, ops=ops: [f(E) for f in ops])
    return nc


_NC_CACHE = [None]


def kernel(**inputs):
    import ml_dtypes
    if _NC_CACHE[0] is None:
        nc = bass.Bass("TRN2", target_bir_lowering=False)
        build(nc)
        _NC_CACHE[0] = nc
    nc = _NC_CACHE[0]
    consts = _host_consts()
    remap = {
        "ln1_w": "L0_attn_ln1_w", "ln1_b": "L0_attn_ln1_b",
        "inpw": "L0_attn_in_proj_w", "convw": "L0_attn_conv_w",
        "convb": "L0_attn_conv_b", "dtb": "L0_attn_dt_bias",
        "alog": "L0_attn_A_log", "Dh": "L0_attn_D", "normw": "L0_attn_norm_w",
        "outpw": "L0_attn_out_proj_w", "ln2_w": "L0_attn_ln2_w",
        "ln2_b": "L0_attn_ln2_b", "mw1": "L0_attn_mlp_w1", "mb1": "L0_attn_mlp_b1",
        "mw2": "L0_attn_mlp_w2", "mb2": "L0_attn_mlp_b2",
        "fqkv": "L0_ffn_Wqkv", "fqs": "L0_ffn_q_scale", "fks": "L0_ffn_k_scale",
        "fwo": "L0_ffn_Wo", "flg": "L0_ffn_ln_g", "flb": "L0_ffn_ln_b",
        "fw1": "L0_ffn_W1", "fb1": "L0_ffn_b1", "fw2": "L0_ffn_W2",
        "fb2": "L0_ffn_b2",
    }
    base = {}
    for nm, src in remap.items():
        base[nm] = np.ascontiguousarray(inputs[src].astype(np.float32))
    base.update(consts)
    x = inputs["L0_attn_x"]
    in_maps = []
    for b in range(8):
        m = dict(base)
        m["x_cm"] = np.ascontiguousarray(x[b].T.astype(np.float32))
        in_maps.append(m)
    res = run_bass_kernel_spmd(nc, in_maps, core_ids=list(range(8)))
    out = np.stack([np.asarray(r["out_cm"]).T for r in res.results], axis=0)
    return out.astype(np.float32)


# version 04 (final)
# speedup vs baseline: 1.0048x; 1.0048x over previous
import numpy as np
from contextlib import ExitStack
import concourse.bass as bass
import concourse.mybir as mybir
from concourse.bass_utils import run_bass_kernel_spmd

F32 = mybir.dt.float32
BF = mybir.dt.bfloat16
AF = mybir.ActivationFunctionType
OP = mybir.AluOpType
AX = mybir.AxisListType.X

T = 4096
D = 1024
TB = 512
NB = T // TB
NEG = -1.0e30
EPS = 1e-5


def _host_consts():
    c = {}
    half = np.arange(0, 64, 2, dtype=np.float64)
    inv_freq = 1.0 / (10000.0 ** (half / 64.0))
    t = np.arange(T, dtype=np.float64)
    freqs = t[:, None] * inv_freq[None, :]
    freqs = np.concatenate([freqs, freqs], -1)
    base_scale = (half + 0.4 * 64.0) / (1.4 * 64.0)
    power = (t - T // 2) / (256 // 2)
    scale = base_scale[None, :] ** power[:, None]
    scale = np.concatenate([scale, scale], -1)
    cos = np.cos(freqs)
    sin = np.sin(freqs)
    for nm, arr in (("CS", cos * scale), ("SS", sin * scale),
                    ("CI", cos / scale), ("SI", sin / scale)):
        a = np.ascontiguousarray(np.tile(arr.T, (2, 1)).astype(np.float32))
        c["c_" + nm] = a
    masks = np.zeros((128, 4 * 512), dtype=np.float32)
    for p in range(2):
        i = np.arange(128)[:, None]
        jj = np.arange(512)[None, :]
        lo = 128 * p + i
        keep = (jj >= lo) & (jj <= lo + 256)
        m = np.where(keep, 0.0, NEG).astype(np.float32)
        masks[:, p * 512:(p + 1) * 512] = m
        m0 = m.copy()
        m0[:, :256] = NEG
        masks[:, (2 + p) * 512:(3 + p) * 512] = m0
    c["c_masks"] = masks
    j = np.arange(128)[:, None]
    i = np.arange(128)[None, :]
    c["c_ltm"] = np.where(i >= j, 0.0, NEG).astype(np.float32)
    c["c_ut"] = np.where(j <= i, 1.0, 0.0).astype(np.float32)
    c["c_if"] = np.eye(128, dtype=np.float32)
    import ml_dtypes
    c["c_ib"] = np.eye(128, dtype=np.float32).astype(ml_dtypes.bfloat16)
    id2 = np.vstack([np.eye(64), np.eye(64)]).astype(np.float32)
    c["c_id2f"] = id2
    c["c_id2b"] = id2.astype(ml_dtypes.bfloat16)
    c["c_onesf"] = np.ones((128, 512), dtype=np.float32)
    c["c_onesb"] = np.ones((128, 512), dtype=np.float32).astype(ml_dtypes.bfloat16)
    s2 = np.zeros((128, 2), dtype=np.float32)
    s2[:64, 0] = 1.0
    s2[64:, 1] = 1.0
    c["c_sel2b"] = s2.astype(ml_dtypes.bfloat16)
    c["c_sel2tb"] = np.ascontiguousarray(s2.T).astype(ml_dtypes.bfloat16)
    R = np.zeros((128, 128), dtype=np.float32)
    for d_ in range(128):
        m = d_ % 64
        b0 = d_ - m
        if m < 32:
            R[d_, b0 + m + 32] = -1.0
        else:
            R[d_, b0 + m - 32] = 1.0
    c["c_rht"] = np.ascontiguousarray(R.T).astype(ml_dtypes.bfloat16)
    c["c_scal"] = np.tile(np.array([1e-5, 1.0, -1.0, 0.0, 0.5, 8.0, 1e-12, 2.0], dtype=np.float32), (128, 1))
    return c


class K:
    def __init__(self, nc):
        self.nc = nc
        self.sem = {e: nc.alloc_semaphore("S" + e) for e in "tvsd"}
        self.cnt = {e: 0 for e in "tvsd"}
        self.wm = {e: {o: 0 for o in "tvsd"} for e in "tvsd"}
        self.buf = {}
        self.q = {e: [] for e in "tvsd"}
        self.nstage = 0

    def _key(self, ap):
        t = getattr(ap, "tensor", None)
        if t is None:
            return str(ap)
        return t.name

    def emit(self, eng, fn, outs, ins):
        need = {}
        for ap in ins:
            rec = self.buf.get(self._key(ap))
            if rec and rec["w"]:
                we, wc = rec["w"]
                if not (we == eng == "t"):
                    need[we] = max(need.get(we, 0), wc)
        for ap in outs:
            rec = self.buf.get(self._key(ap))
            if rec:
                if rec["w"]:
                    we, wc = rec["w"]
                    if not (we == eng == "t"):
                        need[we] = max(need.get(we, 0), wc)
                for re_, rc in rec["r"].items():
                    if re_ != eng and not (re_ == eng == "t"):
                        need[re_] = max(need.get(re_, 0), rc)
                    elif re_ == eng == "d":
                        need[re_] = max(need.get(re_, 0), rc)
        for oe in "tvsd":
            if oe in need and self.wm[eng][oe] < need[oe]:
                v = need[oe]
                self.q[eng].append(lambda E, s=self.sem[oe], vv=v: E.wait_ge(s, vv))
                self.wm[eng][oe] = v
        inc = 16 if eng == "d" else 1
        self.cnt[eng] += inc
        myc = self.cnt[eng]
        self.q[eng].append(lambda E, f=fn, s=self.sem[eng], i=inc: f(E).then_inc(s, i))
        for ap in ins:
            rec = self.buf.setdefault(self._key(ap), {"w": None, "r": {}})
            rec["r"][eng] = myc
        for ap in outs:
            rec = self.buf.setdefault(self._key(ap), {"w": None, "r": {}})
            rec["w"] = (eng, myc)
            rec["r"] = {}

    def t(self, fn, outs, ins):
        self.emit("t", fn, outs, ins)

    def v(self, fn, outs, ins):
        self.emit("v", fn, outs, ins)

    def s(self, fn, outs, ins):
        self.emit("s", fn, outs, ins)

    def dma(self, out, in_):
        out = _ap(out)
        in_ = _ap(in_)
        self.emit("d", lambda E, o=out, i=in_: E.dma_start(o, i), [out], [in_])

    def stage_end(self, name):
        qs = self.q
        self.q = {e: [] for e in "tvsd"}
        blk = self.nc.Block("st%d_%s" % (self.nstage, name))
        self.nstage += 1
        with blk:
            for e, attr in (("d", "sync"), ("t", "tensor"), ("v", "vector"), ("s", "scalar")):
                ops = qs[e]
                if ops:
                    getattr(blk, attr)(lambda E, ops=ops: [f(E) for f in ops])
        self.buf = {}
        for e in "tvsd":
            for o in "tvsd":
                if o != e and self.cnt[o] > self.wm[e][o]:
                    v = self.cnt[o]
                    self.q[e].append(lambda E, s=self.sem[o], vv=v: E.wait_ge(s, vv))
                    self.wm[e][o] = v


def _ap(x):
    if x is None or isinstance(x, (int, float, tuple)):
        return x
    if isinstance(x, bass.AP):
        return x
    return x[:]


def _aps(*xs):
    return [_ap(x) for x in xs]


def MM(k, out, lhsT, rhs, start, stop, tp=None):
    out, lhsT, rhs = _aps(out, lhsT, rhs)
    k.t(lambda E, o=out, l=lhsT, r=rhs, a=start, b=stop, t_=tp:
        E.matmul(o, l, r, start=a, stop=b, tile_position=t_), [out], [lhsT, rhs])


def TR(k, out, in_, ident, tp=None):
    out, in_, ident = _aps(out, in_, ident)
    k.t(lambda E, o=out, i=in_, d=ident, t_=tp: E.matmul(o, i, d, start=True, stop=True,
        is_transpose=True, tile_position=t_), [out], [in_, ident])


def TT(k, out, a, b, op):
    out, a, b = _aps(out, a, b)
    k.v(lambda E, o=out, x=a, y=b, p=op: E.tensor_tensor(o, x, y, p), [out], [a, b])


def TS(k, out, in_, s1, op, s2=None, op2=None):
    out, in_, s1, s2 = _aps(out, in_, s1, s2)
    ins = [in_] + [x for x in (s1, s2) if hasattr(x, "tensor")]
    if s2 is None:
        k.v(lambda E, o=out, i=in_, a=s1, p=op: E.tensor_scalar(o, i, a, None, p),
            [out], ins)
    else:
        k.v(lambda E, o=out, i=in_, a=s1, b=s2, p=op, q=op2:
            E.tensor_scalar(o, i, a, b, p, q), [out], ins)


def STT(k, out, in0, sc, in1, op0, op1):
    out, in0, sc, in1 = _aps(out, in0, sc, in1)
    ins = [in0, in1] + ([sc] if hasattr(sc, "tensor") else [])
    k.v(lambda E, o=out, a=in0, s=sc, b=in1, p=op0, q=op1:
        E.scalar_tensor_tensor(o, a, s, b, p, q), [out], ins)


def CP(k, out, in_):
    out, in_ = _aps(out, in_)
    k.v(lambda E, o=out, i=in_: E.tensor_copy(o, i), [out], [in_])


def RED(k, out, in_, op, negate=False):
    out, in_ = _aps(out, in_)
    k.v(lambda E, o=out, i=in_, p=op, n=negate:
        E.tensor_reduce(o, i, AX, p, negate=n), [out], [in_])


def RECIP(k, out, in_):
    out, in_ = _aps(out, in_)
    k.v(lambda E, o=out, i=in_: E.reciprocal(o, i), [out], [in_])


def ACT(k, out, in_, func, bias=0.0, scale=1.0):
    out, in_, bias, scale = _aps(out, in_, bias, scale)
    ins = [in_] + [x for x in (bias, scale) if hasattr(x, "tensor")]
    k.s(lambda E, o=out, i=in_, f=func, b=bias, sc=scale:
        E.activation(o, i, f, bias=b, scale=sc), [out], ins)


def SMUL(k, out, in_, m):
    out, in_ = _aps(out, in_)
    k.s(lambda E, o=out, i=in_, mm_=m: E.mul(o, i, mm_), [out], [in_])


def SCP(k, out, in_):
    out, in_ = _aps(out, in_)
    k.s(lambda E, o=out, i=in_: E.copy(o, i), [out], [in_])


def MEMSET(k, ap, val=0.0):
    ap = _ap(ap)
    k.v(lambda E, a=ap, v=val: E.memset(a, v), [ap], [])


def build(nc):
    with nc.allow_low_precision(reason="bf16 PE transposes are pure permutations"):
        return _build(nc)


def _build(nc):
    k = K(nc)
    _uid = [0]

    def _sbt(name, shape, dtype):
        _uid[0] += 1
        return nc.sbuf_tensor("%s_u%d" % (name, _uid[0]), shape, dtype)

    def _pst(name, shape, dtype):
        _uid[0] += 1
        return nc.psum_tensor("%s_u%d" % (name, _uid[0]), shape, dtype)
    dram = {}

    def din(name, shape, dtype=F32):
        dram[name] = nc.dram_tensor(name, list(shape), dtype, kind="ExternalInput")
        return dram[name]

    def dint(name, shape, dtype=F32):
        dram[name] = nc.dram_tensor(name, list(shape), dtype, kind="Internal")
        return dram[name]

    din("x_cm", [D, T])
    din("ln1_w", [D]); din("ln1_b", [D])
    din("inpw", [4256, D])
    din("convw", [2176, 4]); din("convb", [2176])
    din("dtb", [32]); din("alog", [32]); din("Dh", [32])
    din("normw", [2048])
    din("outpw", [D, 2048])
    din("ln2_w", [D]); din("ln2_b", [D])
    din("mw1", [4096, D]); din("mb1", [4096])
    din("mw2", [D, 4096]); din("mb2", [D])
    din("fqkv", [6, D, 3072])
    din("fqs", [6, 64]); din("fks", [6, 64])
    din("fwo", [6, D, D])
    din("flg", [6, D]); din("flb", [6, D])
    din("fw1", [6, D, 5460]); din("fb1", [6, 5460])
    din("fw2", [6, 2730, D]); din("fb2", [6, D])
    for nm in ("c_CS", "c_SS", "c_CI", "c_SI"):
        din(nm, [128, T])
    din("c_masks", [128, 2048])
    din("c_ltm", [128, 128]); din("c_ut", [128, 128])
    din("c_if", [128, 128]); din("c_ib", [128, 128], BF)
    din("c_id2f", [128, 64]); din("c_id2b", [128, 64], BF)
    din("c_onesf", [128, 512]); din("c_onesb", [128, 512], BF)
    din("c_sel2b", [128, 2], BF); din("c_sel2tb", [2, 128], BF)
    din("c_rht", [128, 128], BF)
    din("c_scal", [128, 8])
    out_t = nc.dram_tensor("out_cm", [D, T], F32, kind="ExternalOutput")
    dram["out_cm"] = out_t

    dint("xr", [D, T]); dint("u", [D, T]); dint("t0", [D, T]); dint("t1", [D, T])
    dint("zx", [4256, T])
    dint("xbcs", [2176, T], BF)
    dint("dts", [T, 32])
    dint("ymm", [2048, T], BF)
    dint("yn", [2048, T], BF)
    dint("g2d", [2816, T], BF)
    dint("m1d", [4096, T], BF)
    dint("xnb", [D, T], BF)
    dint("tp", [D, T])

    def A(name):
        return dram[name].ap()

    es_glob = ExitStack()

    def sb(name, shape, dtype):
        return es_glob.enter_context(nc.sbuf_tensor(name, list(shape), dtype))

    IF = sb("IF", [128, 128], F32)
    IB = sb("IB", [128, 128], BF)
    ID2F = sb("ID2F", [128, 64], F32)
    ID2B = sb("ID2B", [128, 64], BF)
    ONESF = sb("ONESF", [128, 512], F32)
    ONESB = sb("ONESB", [128, 512], BF)
    SEL2B = sb("SEL2B", [128, 2], BF)
    SEL2TB = sb("SEL2TB", [2, 128], BF)
    RHT = sb("RHT", [128, 128], BF)
    UTF = sb("UTF", [128, 128], F32)
    LTM = sb("LTM", [128, 128], F32)
    MASKS = sb("MASKS", [128, 2048], F32)
    COLS = sb("COLS", [128, 640], F32)
    CSC = sb("CSC", [128, 8], F32)
    _scal_vals = [EPS, 1.0, -1.0, 0.0, 0.5, 8.0, 1e-12, 2.0]
    for _i, _v in enumerate(_scal_vals):
        nc.const_aps.register(F32, _v, CSC[:, _i:_i + 1])

    CL = {}
    _c = [0]

    def cslot(nm, n):
        CL[nm] = _c[0]
        _c[0] += n

    cslot("ln1w", 8); cslot("ln1b", 8); cslot("ln2w", 8); cslot("ln2b", 8)
    cslot("normw", 16); cslot("mb1", 32); cslot("mb2", 8)
    cslot("dtb", 1); cslot("Dh", 1)
    for i in range(6):
        cslot("flg%d" % i, 8); cslot("flb%d" % i, 8)
        cslot("b1a%d" % i, 22); cslot("b1g%d" % i, 22); cslot("fb2_%d" % i, 8)
        cslot("fqs%d" % i, 1); cslot("fks%d" % i, 1)
    assert _c[0] <= 640

    k.dma(CSC[:, 0:8], A("c_scal")[:, :])
    for nm, dst in (("c_if", IF), ("c_ib", IB), ("c_id2f", ID2F), ("c_id2b", ID2B),
                    ("c_onesf", ONESF), ("c_onesb", ONESB), ("c_sel2b", SEL2B),
                    ("c_sel2tb", SEL2TB), ("c_rht", RHT), ("c_ut", UTF),
                    ("c_ltm", LTM), ("c_masks", MASKS)):
        sh = dram[nm].shape
        k.dma(dst[0:sh[0], 0:sh[1]], A(nm)[0:sh[0], 0:sh[1]])

    _es = ExitStack()
    lcstg = _es.enter_context(_sbt("lcstg", [128, 128], F32))
    lcps = _es.enter_context(_pst("lcps", [128, 64], F32))
    if True:

        def load_col(dst_cols, src1d, C, scale=1.0):
            rows = C // 128
            tail = C % 128
            n = rows + (1 if tail else 0)
            if rows:
                k.dma(lcstg[0:rows, 0:128], src1d.rearrange("(r c) -> r c", c=128)
                      if False else src1d[0:rows * 128].rearrange("(r c) -> r c", c=128))
            if tail:
                k.dma(lcstg[rows:rows + 1, 0:tail],
                      src1d[rows * 128:rows * 128 + tail][None, :])
            TR(k, lcps[0:128, 0:n], lcstg[0:n, 0:128], IF[0:n, 0:n])
            if scale == 1.0:
                CP(k, dst_cols[:, 0:n], lcps[0:128, 0:n])
            else:
                SMUL(k, dst_cols[:, 0:n], lcps[0:128, 0:n], scale)

        def colsl(nm, n):
            return COLS[:, CL[nm]:CL[nm] + n]

        load_col(colsl("ln1w", 8), A("ln1_w"), D)
        load_col(colsl("ln1b", 8), A("ln1_b"), D)
        load_col(colsl("ln2w", 8), A("ln2_w"), D)
        load_col(colsl("ln2b", 8), A("ln2_b"), D)
        load_col(colsl("normw", 16), A("normw"), 2048)
        load_col(colsl("mb1", 32), A("mb1"), 4096)
        load_col(colsl("mb2", 8), A("mb2"), D)
        load_col(colsl("dtb", 1), A("dtb"), 32)
        load_col(colsl("Dh", 1), A("Dh"), 32)
        for i in range(6):
            load_col(colsl("flg%d" % i, 8), A("flg")[i], D)
            load_col(colsl("flb%d" % i, 8), A("flb")[i], D)
            load_col(colsl("b1a%d" % i, 22), A("fb1")[i][0:2730], 2730)
            load_col(colsl("b1g%d" % i, 22), A("fb1")[i][2730:5460], 2730)
            load_col(colsl("fb2_%d" % i, 8), A("fb2")[i], D)
            for nm2, src, sc in (("fqs%d" % i, A("fqs")[i], 8.0),
                                 ("fks%d" % i, A("fks")[i], 1.0)):
                k.dma(lcstg[0:1, 0:64], src[0:64][None, :])
                k.dma(lcstg[0:1, 64:128], src[0:64][None, :])
                TR(k, lcps[0:128, 0:1], lcstg[0:1, 0:128], IF[0:1, 0:1])
                if sc == 1.0:
                    CP(k, colsl(nm2, 1), lcps[0:128, 0:1])
                else:
                    SMUL(k, colsl(nm2, 1), lcps[0:128, 0:1], sc)
        k.stage_end("prep")
        _es.close()

    def colsl(nm, n):
        return COLS[:, CL[nm]:CL[nm] + n]

    def ln_pass(name, src, dst, gcol=None, bcol=None):
        _es = ExitStack()
        X = _es.enter_context(_sbt("lnx", [128, 4096], F32))
        SQ = _es.enter_context(_sbt("lnsq", [128, 512], BF))
        XBC = _es.enter_context(_sbt("lnxb", [128, 512], BF))
        ST = _es.enter_context(_sbt("lnst", [1, 2048], F32))
        O = _es.enter_context(_sbt("lnout", [128, 4096], F32))
        P1 = _es.enter_context(_pst("lnp1", [1, 512], F32))
        P2 = _es.enter_context(_pst("lnp2", [1, 512], F32))
        P3 = _es.enter_context(_pst("lnp3", [128, 512], F32))
        P4 = _es.enter_context(_pst("lnp4", [128, 512], F32))
        if True:
            for b in range(NB):
                t0 = b * TB
                for c in range(8):
                    k.dma(X[:, c * 512:(c + 1) * 512], src[c * 128:(c + 1) * 128, t0:t0 + TB])
                for c in range(8):
                    xc = X[:, c * 512:(c + 1) * 512]
                    CP(k, XBC, xc)
                    MM(k, P1, ONESB[:, 0:1], XBC, c == 0, c == 7)
                    ACT(k, SQ, xc, AF.Square)
                    MM(k, P2, ONESB[:, 0:1], SQ, c == 0, c == 7)
                mu = ST[0:1, 0:512]; var = ST[0:1, 512:1024]
                rstd = ST[0:1, 1024:1536]; musq = ST[0:1, 1536:2048]
                TS(k, mu, P1[0:1, :], 1.0 / D, OP.mult)
                TS(k, var, P2[0:1, :], 1.0 / D, OP.mult)
                TT(k, musq, mu, mu, OP.mult)
                TT(k, var, var, musq, OP.subtract)
                ACT(k, rstd, var, AF.Sqrt, bias=EPS)
                RECIP(k, rstd, rstd)
                MM(k, P3, ONESF[0:1, 0:128], mu, True, True)
                MM(k, P4, ONESF[0:1, 0:128], rstd, True, True)
                for c in range(8):
                    xc = X[:, c * 512:(c + 1) * 512]
                    oc = O[:, c * 512:(c + 1) * 512]
                    TT(k, oc, xc, P3, OP.subtract)
                    TT(k, oc, oc, P4, OP.mult)
                    if gcol is not None:
                        TS(k, oc, oc, gcol[:, c:c + 1], OP.mult,
                           bcol[:, c:c + 1], OP.add)
                for c in range(8):
                    k.dma(dst[c * 128:(c + 1) * 128, t0:t0 + TB], O[:, c * 512:(c + 1) * 512])
            k.stage_end(name)
            _es.close()

    def axpy_pass(name, src_x, src_y, dst, alpha):
        _es = ExitStack()
        X = _es.enter_context(_sbt("apx", [128, 4096], F32))
        Y = _es.enter_context(_sbt("apy", [128, 4096], F32))
        O = _es.enter_context(_sbt("apo", [128, 4096], F32))
        if True:
            for b in range(NB):
                t0 = b * TB
                for c in range(8):
                    k.dma(X[:, c * 512:(c + 1) * 512], src_x[c * 128:(c + 1) * 128, t0:t0 + TB])
                    k.dma(Y[:, c * 512:(c + 1) * 512], src_y[c * 128:(c + 1) * 128, t0:t0 + TB])
                for c in range(8):
                    sl = slice(c * 512, (c + 1) * 512)
                    if alpha == 1.0:
                        TT(k, O[:, sl], X[:, sl], Y[:, sl], OP.add)
                    else:
                        STT(k, O[:, sl], Y[:, sl], alpha, X[:, sl], OP.mult, OP.add)
                for c in range(8):
                    k.dma(dst[c * 128:(c + 1) * 128, t0:t0 + TB], O[:, c * 512:(c + 1) * 512])
            k.stage_end(name)
            _es.close()

    def load_w_inout(W0, wap, kchunks, ncols, stg, coloff=0):
        stgw = stg.shape[1] if hasattr(stg, "shape") else 1024
        stgw = 1024
        for c in range(kchunks):
            for p0 in range(0, ncols, stgw):
                pw = min(stgw, ncols - p0)
                k.dma(stg[:, 0:pw], wap[c * 128:(c + 1) * 128, p0:p0 + pw])
                CP(k, W0[:, coloff + c * ncols + p0:coloff + c * ncols + p0 + pw],
                   stg[:, 0:pw])

    def load_w_T(W0, wap, out_rows, in_cols, stg, stgb, coloff=0, row0=0):
        nk = in_cols // 128
        nm = out_rows // 128
        with _pst("jtp", [128, 512], BF) as JP:
            for m in range(nm):
                k.dma(stg[:, 0:in_cols], wap[row0 + m * 128:row0 + (m + 1) * 128, 0:in_cols])
                CP(k, stgb[:, 0:in_cols], stg[:, 0:in_cols])
                for c in range(nk):
                    pslot = c % 4
                    TR(k, JP[:, pslot * 128:(pslot + 1) * 128],
                       stgb[:, c * 128:(c + 1) * 128], IB)
                    CP(k, W0[:, coloff + c * out_rows + m * 128:
                             coloff + c * out_rows + (m + 1) * 128],
                       JP[:, pslot * 128:(pslot + 1) * 128])

    def attn_stage(li, src, dst):
        _es = ExitStack()
        W0 = _es.enter_context(_sbt("aw0", [128, 8 * 3072 + 8 * 1024], BF))
        STG = _es.enter_context(_sbt("astg", [128, 1024], F32))
        XIN = _es.enter_context(_sbt("axin", [128, 4096], F32))
        XU1 = _es.enter_context(_sbt("axu1", [128, 512], F32))
        XB = _es.enter_context(_sbt("axb", [128, 4096], BF))
        Q = _es.enter_context(_sbt("aq", [128, 4096], F32))
        KK = _es.enter_context(_sbt("ak", [128, 4096], F32))
        QR = _es.enter_context(_sbt("aqr", [128, 4096], BF))
        KW = _es.enter_context(_sbt("akw", [128, 8 * 768], BF))
        VW = _es.enter_context(_sbt("avw", [128, 6 * 1024], BF))
        AO = _es.enter_context(_sbt("aao", [128, 4096], BF))
        ROT = _es.enter_context(_sbt("arot", [128, 2048], F32))
        SMs = [_es.enter_context(_sbt("asm", [128, 512], F32)) for _ in range(2)]
        PMs = [_es.enter_context(_sbt("apm", [128, 512], F32)) for _ in range(2)]
        PBs = [_es.enter_context(_sbt("apb", [128, 512], BF)) for _ in range(2)]
        PTs = [_es.enter_context(_sbt("apt", [128, 512], BF)) for _ in range(2)]
        NRM = _es.enter_context(_sbt("anrm", [2, 512], F32))
        RR = _es.enter_context(_sbt("arr", [2, 512], BF))
        STATs = [_es.enter_context(_sbt("astat", [128, 4], F32)) for _ in range(2)]
        OF = _es.enter_context(_sbt("aof", [128, 512], F32))
        PS0s = [_es.enter_context(_pst("aps0", [128, 512], F32)) for _ in range(2)]
        PSIMs = [_es.enter_context(_pst("apsim", [128, 512], F32)) for _ in range(2)]
        PSPTs = [_es.enter_context(_pst("apspt", [128, 512], BF)) for _ in range(2)]
        PSVT = _es.enter_context(_pst("apsvt", [128, 1024], BF))
        PSAUX = _es.enter_context(_pst("apsaux", [128, 512], F32))
        PSROT = PSAUX
        PSNN = PSAUX
        PSREP = PSAUX
        PSAV = PSAUX
        if True:
            load_w_inout(W0, A("fqkv")[li], 8, 3072, STG)
            load_w_inout(W0, A("fwo")[li], 8, 1024, STG, coloff=8 * 3072)
            WOOFF = 8 * 3072
            qcol = colsl("fqs%d" % li, 1)
            kcol = colsl("fks%d" % li, 1)
            for b in range(NB):
                t0 = b * TB
                for c in range(4):
                    k.dma(XIN[:, c * 512:(c + 1) * 512],
                          src[c * 128:(c + 1) * 128, t0:t0 + TB])
                for c in range(4, 8):
                    if b == 0:
                        MEMSET(k, XIN[:, c * 512:c * 512 + 1], 0.0)
                        k.dma(XIN[:, c * 512 + 1:(c + 1) * 512],
                              src[c * 128:(c + 1) * 128, 0:TB - 1])
                    else:
                        k.dma(XIN[:, c * 512:(c + 1) * 512],
                              src[c * 128:(c + 1) * 128, t0 - 1:t0 + TB - 1])
                for j, nm in enumerate(("c_CS", "c_SS", "c_CI", "c_SI")):
                    k.dma(ROT[:, j * 512:(j + 1) * 512], A(nm)[:, t0:t0 + TB])
                CP(k, XB, XIN)
                if b == 0:
                    MEMSET(k, KW[:, 0:8 * 768], 0.0)
                    MEMSET(k, VW[:, 0:2048], 0.0)
                else:
                    k.v(lambda E, o=KW, i=KW: E.tensor_copy(
                        o.rearrange("p (c w) -> p c w", w=768)[:, :, 0:256],
                        i.rearrange("p (c w) -> p c w", w=768)[:, :, 512:768]),
                        [KW], [KW])
                    CP(k, VW[:, 0:2048], VW[:, 4096:6144])
                for m in range(24):
                    PS0 = PS0s[m % 2]
                    for c in range(8):
                        MM(k, PS0, W0[:, c * 3072 + m * 128:c * 3072 + (m + 1) * 128],
                           XB[:, c * 512:(c + 1) * 512], c == 0, c == 7)
                    if m < 8:
                        (CP if m % 2 else SCP)(k, Q[:, m * 512:(m + 1) * 512], PS0)
                    elif m < 16:
                        (CP if m % 2 else SCP)(k, KK[:, (m - 8) * 512:(m - 7) * 512], PS0)
                    else:
                        (CP if m % 2 else SCP)(k, AO[:, (m - 16) * 512:(m - 15) * 512], PS0)
                for (SRC, ccol) in ((Q, qcol), (KK, kcol)):
                    for c in range(8):
                        xc = SRC[:, c * 512:(c + 1) * 512]
                        ACT(k, PBs[0], xc, AF.Square)
                        MM(k, PSNN[0:2, :], SEL2B, PBs[0], True, True)
                        ACT(k, NRM, PSNN[0:2, :], AF.Sqrt)
                        TS(k, NRM, NRM, 1e-12, OP.max)
                        RECIP(k, RR, NRM)
                        MM(k, PSREP, SEL2TB, RR, True, True)
                        STT(k, xc, xc, ccol, PSREP, OP.mult, OP.mult)
                for (SRC, COSO, SINO, DST) in ((Q, 0, 512, None), (KK, 1024, 1536, None)):
                    for c in range(8):
                        xc = SRC[:, c * 512:(c + 1) * 512]
                        SCP(k, PBs[c % 2], xc)
                        MM(k, PSROT, RHT, PBs[c % 2], True, True)
                        TT(k, SMs[c % 2], xc, ROT[:, COSO:COSO + 512], OP.mult)
                        TT(k, PMs[c % 2], PSROT, ROT[:, SINO:SINO + 512], OP.mult)
                        if SRC is Q:
                            TT(k, QR[:, c * 512:(c + 1) * 512], SMs[c % 2], PMs[c % 2], OP.add)
                        else:
                            TT(k, KW[:, c * 768 + 256:c * 768 + 768], SMs[c % 2], PMs[c % 2], OP.add)
                for ti in range(4):
                    for m in range(8):
                        TR(k, PSVT[:, m * 128:(m + 1) * 128],
                           AO[:, m * 512 + ti * 128:m * 512 + (ti + 1) * 128], IB)
                    CP(k, VW[:, (2 + ti) * 1024:(3 + ti) * 1024], PSVT)
                it = 0
                for wl in range(2):
                    wg = 2 * b + wl
                    for h in range(16):
                        c = h // 2
                        pb_ = 64 * (h % 2)
                        for ic in range(2):
                            J = it % 2
                            it += 1
                            PSIM = PSIMs[J]; SM = SMs[J]; PM = PMs[J]
                            PB = PBs[J]; PT = PTs[J]; STAT = STATs[J]
                            PSPT = PSPTs[J]
                            qsl = QR[pb_:pb_ + 64,
                                     c * 512 + wl * 256 + ic * 128:
                                     c * 512 + wl * 256 + (ic + 1) * 128]
                            ksl = KW[pb_:pb_ + 64, c * 768 + wl * 256:c * 768 + wl * 256 + 512]
                            MM(k, PSIM, qsl, ksl, True, True)
                            moff = ((2 if wg == 0 else 0) + ic) * 512
                            TT(k, SM, PSIM, MASKS[:, moff:moff + 512], OP.add)
                            RED(k, STAT[:, 0:1], SM, OP.max, negate=True)
                            ACT(k, PM, SM, AF.Exp, bias=STAT[:, 0:1])
                            RED(k, STAT[:, 1:2], PM, OP.add)
                            RECIP(k, STAT[:, 2:3], STAT[:, 1:2])
                            TS(k, PB, PM, STAT[:, 2:3], OP.mult)
                            for jc in range(4):
                                TR(k, PSPT[:, jc * 128:(jc + 1) * 128],
                                   PB[:, jc * 128:(jc + 1) * 128], IB)
                            SCP(k, PT, PSPT)
                            tp = None if pb_ == 0 else (0, 64)
                            for jc in range(4):
                                rt_ = wl * 2 + jc
                                MM(k, PSAV[pb_:pb_ + 64, 384:512],
                                   VW[:, rt_ * 1024 + h * 64:rt_ * 1024 + (h + 1) * 64],
                                   PT[:, jc * 128:(jc + 1) * 128], jc == 0, jc == 3,
                                   tp=tp)
                            SCP(k, AO[pb_:pb_ + 64,
                                      c * 512 + wl * 256 + ic * 128:
                                      c * 512 + wl * 256 + (ic + 1) * 128],
                                PSAV[pb_:pb_ + 64, 384:512])
                for m in range(8):
                    PS0 = PS0s[m % 2]
                    for c in range(8):
                        MM(k, PS0, W0[:, WOOFF + c * 1024 + m * 128:
                                      WOOFF + c * 1024 + (m + 1) * 128],
                           AO[:, c * 512:(c + 1) * 512], c == 0, c == 7)
                    if m < 4:
                        res = XIN[:, m * 512:(m + 1) * 512]
                    else:
                        k.dma(XU1, src[m * 128:(m + 1) * 128, t0:t0 + TB])
                        res = XU1
                    TT(k, OF if m % 2 == 0 else SMs[0], PS0, res, OP.add)
                    k.dma(dst[m * 128:(m + 1) * 128, t0:t0 + TB],
                          OF if m % 2 == 0 else SMs[0])
            k.stage_end("attn%d" % li)
            _es.close()

    def ffn1_stage(li, src, half):
        pairs = list(range(0, 11)) if half == 0 else list(range(11, 22))
        _es = ExitStack()
        W0 = _es.enter_context(_sbt("fw0s", [128, 8 * 2816], BF))
        STG = _es.enter_context(_sbt("fstg", [128, 2816], F32))
        XIN = _es.enter_context(_sbt("fxin", [128, 4096], F32))
        XB = _es.enter_context(_sbt("fxb", [128, 4096], BF))
        SQ = _es.enter_context(_sbt("fsq", [128, 512], BF))
        ST = _es.enter_context(_sbt("fst", [1, 2048], F32))
        AFB = _es.enter_context(_sbt("faf", [128, 512], F32))
        GG = _es.enter_context(_sbt("fgg", [128, 512], F32))
        G2 = _es.enter_context(_sbt("fg2", [128, 512], BF))
        P1 = _es.enter_context(_pst("fp1", [1, 512], F32))
        P2 = _es.enter_context(_pst("fp2", [1, 512], F32))
        P3 = _es.enter_context(_pst("fp3", [128, 512], F32))
        P4 = _es.enter_context(_pst("fp4", [128, 512], F32))
        PA = _es.enter_context(_pst("fpa", [128, 512], F32))
        PG = _es.enter_context(_pst("fpg", [128, 512], F32))
        if True:
            for c in range(8):
                wap = A("fw1")[li]
                for idx, p in enumerate(pairs):
                    w = min(128, 2730 - p * 128)
                    k.dma(STG[:, idx * 128:idx * 128 + w],
                          wap[c * 128:(c + 1) * 128, p * 128:p * 128 + w])
                    k.dma(STG[:, 1408 + idx * 128:1408 + idx * 128 + w],
                          wap[c * 128:(c + 1) * 128, 2730 + p * 128:2730 + p * 128 + w])
                CP(k, W0[:, c * 2816:(c + 1) * 2816], STG[:, 0:2816])
            gcol = colsl("flg%d" % li, 8)
            bcol = colsl("flb%d" % li, 8)
            b1a = colsl("b1a%d" % li, 22)
            b1g = colsl("b1g%d" % li, 22)
            for b in range(NB):
                t0 = b * TB
                if half == 0:
                    for c in range(4):
                        k.dma(XIN[:, c * 512:(c + 1) * 512],
                              src[c * 128:(c + 1) * 128, t0:t0 + TB])
                    for c in range(4, 8):
                        if b == 0:
                            MEMSET(k, XIN[:, c * 512:c * 512 + 1], 0.0)
                            k.dma(XIN[:, c * 512 + 1:(c + 1) * 512],
                                  src[c * 128:(c + 1) * 128, 0:TB - 1])
                        else:
                            k.dma(XIN[:, c * 512:(c + 1) * 512],
                                  src[c * 128:(c + 1) * 128, t0 - 1:t0 + TB - 1])
                    for c in range(8):
                        xc = XIN[:, c * 512:(c + 1) * 512]
                        CP(k, SQ, xc)
                        MM(k, P1, ONESB[:, 0:1], SQ, c == 0, c == 7)
                        ACT(k, SQ, xc, AF.Square)
                        MM(k, P2, ONESB[:, 0:1], SQ, c == 0, c == 7)
                    mu = ST[0:1, 0:512]; var = ST[0:1, 512:1024]
                    rstd = ST[0:1, 1024:1536]; musq = ST[0:1, 1536:2048]
                    TS(k, mu, P1[0:1, :], 1.0 / D, OP.mult)
                    TS(k, var, P2[0:1, :], 1.0 / D, OP.mult)
                    TT(k, musq, mu, mu, OP.mult)
                    TT(k, var, var, musq, OP.subtract)
                    ACT(k, rstd, var, AF.Sqrt, bias=EPS)
                    RECIP(k, rstd, rstd)
                    MM(k, P3, ONESF[0:1, 0:128], mu, True, True)
                    MM(k, P4, ONESF[0:1, 0:128], rstd, True, True)
                    for c in range(8):
                        xc = XIN[:, c * 512:(c + 1) * 512]
                        TT(k, xc, xc, P3, OP.subtract)
                        TT(k, xc, xc, P4, OP.mult)
                        TS(k, XB[:, c * 512:(c + 1) * 512], xc,
                           gcol[:, c:c + 1], OP.mult, bcol[:, c:c + 1], OP.add)
                        k.dma(A("xnb")[c * 128:(c + 1) * 128, t0:t0 + TB],
                              XB[:, c * 512:(c + 1) * 512])
                else:
                    for c in range(8):
                        k.dma(XB[:, c * 512:(c + 1) * 512],
                              A("xnb")[c * 128:(c + 1) * 128, t0:t0 + TB])
                for idx, p in enumerate(pairs):
                    w = min(128, 2730 - p * 128)
                    for c in range(8):
                        MM(k, PA[0:w, :], W0[:, c * 2816 + idx * 128:c * 2816 + idx * 128 + w],
                           XB[:, c * 512:(c + 1) * 512], c == 0, c == 7)
                    for c in range(8):
                        MM(k, PG[0:w, :],
                           W0[:, c * 2816 + 1408 + idx * 128:c * 2816 + 1408 + idx * 128 + w],
                           XB[:, c * 512:(c + 1) * 512], c == 0, c == 7)
                    TS(k, AFB[0:w, :], PA[0:w, :], b1a[0:w, p:p + 1], OP.add)
                    ACT(k, GG[0:w, :], PG[0:w, :], AF.Gelu, bias=b1g[0:w, p:p + 1])
                    TT(k, G2[0:w, :], AFB[0:w, :], GG[0:w, :], OP.mult)
                    k.dma(A("g2d")[p * 128:p * 128 + w, t0:t0 + TB], G2[0:w, :])
            k.stage_end("ffn1_%d_%d" % (li, half))
            _es.close()

    def ffn2_stage(li, res, dst):
        _es = ExitStack()
        W0 = _es.enter_context(_sbt("f2w0", [128, 22 * 1024], BF))
        STG = _es.enter_context(_sbt("f2stg", [128, 1024], F32))
        G2 = _es.enter_context(_sbt("f2g", [128, 22 * 512], BF))
        RS = _es.enter_context(_sbt("f2r", [128, 4096], F32))
        O = _es.enter_context(_sbt("f2o", [128, 512], F32))
        PS = _es.enter_context(_pst("f2p", [128, 512], F32))
        if True:
            wap = A("fw2")[li]
            for c in range(22):
                w = min(128, 2730 - c * 128)
                k.dma(STG[0:w, 0:1024], wap[c * 128:c * 128 + w, 0:1024])
                CP(k, W0[0:w, c * 1024:(c + 1) * 1024], STG[0:w, 0:1024])
            b2 = colsl("fb2_%d" % li, 8)
            for b in range(NB):
                t0 = b * TB
                for c in range(22):
                    w = min(128, 2730 - c * 128)
                    k.dma(G2[0:w, c * 512:(c + 1) * 512],
                          A("g2d")[c * 128:c * 128 + w, t0:t0 + TB])
                for c in range(8):
                    k.dma(RS[:, c * 512:(c + 1) * 512], res[c * 128:(c + 1) * 128, t0:t0 + TB])
                for m in range(8):
                    for c in range(22):
                        w = min(128, 2730 - c * 128)
                        MM(k, PS, W0[0:w, c * 1024 + m * 128:c * 1024 + (m + 1) * 128],
                           G2[0:w, c * 512:(c + 1) * 512], c == 0, c == 21)
                    STT(k, O, PS, b2[:, m:m + 1], RS[:, m * 512:(m + 1) * 512],
                        OP.add, OP.add)
                    k.dma(dst[m * 128:(m + 1) * 128, t0:t0 + TB], O)
            k.stage_end("ffn2_%d" % li)
            _es.close()

    def inproj_stage(half):
        r0 = 0 if half == 0 else 2176
        nm_ = 17 if half == 0 else 16
        rows = 2176 if half == 0 else 2080
        nmch = (rows + 127) // 128
        _es = ExitStack()
        W0 = _es.enter_context(_sbt("ipw0", [128, 8 * 2176], BF))
        STG = _es.enter_context(_sbt("ipstg", [128, 1024], F32))
        STGB = _es.enter_context(_sbt("ipstgb", [128, 1024], BF))
        XIN = _es.enter_context(_sbt("ipx", [128, 4096], F32))
        XB = _es.enter_context(_sbt("ipxb", [128, 4096], BF))
        O = _es.enter_context(_sbt("ipo", [128, 512], F32))
        PS = _es.enter_context(_pst("ipp", [128, 512], F32))
        if True:
            with _pst("ipjt", [128, 512], BF) as JP:
                for m in range(nmch):
                    w = min(128, rows - m * 128)
                    k.dma(STG[0:w, 0:1024], A("inpw")[r0 + m * 128:r0 + m * 128 + w, 0:1024])
                    CP(k, STGB[0:w, 0:1024], STG[0:w, 0:1024])
                    for c in range(8):
                        pslot = c % 4
                        TR(k, JP[:, pslot * 128:pslot * 128 + w],
                           STGB[0:w, c * 128:(c + 1) * 128], IB[0:w, 0:w])
                        CP(k, W0[:, c * rows + m * 128:c * rows + m * 128 + w],
                           JP[:, pslot * 128:pslot * 128 + w])
                for b in range(NB):
                    t0 = b * TB
                    for c in range(8):
                        k.dma(XIN[:, c * 512:(c + 1) * 512],
                              A("t1")[c * 128:(c + 1) * 128, t0:t0 + TB])
                    CP(k, XB, XIN)
                    for m in range(nmch):
                        w = min(128, rows - m * 128)
                        for c in range(8):
                            MM(k, PS[0:w, :], W0[:, c * rows + m * 128:c * rows + m * 128 + w],
                               XB[:, c * 512:(c + 1) * 512], c == 0, c == 7)
                        CP(k, O[0:w, :], PS[0:w, :])
                        k.dma(A("zx")[r0 + m * 128:r0 + m * 128 + w, t0:t0 + TB], O[0:w, :])
            k.stage_end("inproj%d" % half)
            _es.close()

    def convdt_stage():
        HB = 256
        _es = ExitStack()
        CW = _es.enter_context(_sbt("cvw", [128, 17 * 4], F32))
        CB = _es.enter_context(_sbt("cvb", [128, 17], F32))
        CSTG = _es.enter_context(_sbt("cvstg", [17, 512], F32))
        XIN = _es.enter_context(_sbt("cvx", [128, 17 * 259], F32))
        ACC = _es.enter_context(_sbt("cvacc", [128, 256], F32))
        SIL = _es.enter_context(_sbt("cvsil", [128, 17 * 256], BF))
        DTI = _es.enter_context(_sbt("cvdt", [32, 256], F32))
        DW = _es.enter_context(_sbt("cvdw", [32, 256], F32))
        DW2 = _es.enter_context(_sbt("cvd2", [32, 256], F32))
        CVP = _es.enter_context(_pst("cvp", [128, 64], F32))
        CVPT = _es.enter_context(_pst("cvpt", [128, 64], F32))
        if True:
            k.dma(CSTG[0:17, 0:512], A("convw").rearrange("(r p) k -> r (p k)", p=128))
            for tap in range(4):
                TR(k, CVP[:, 0:17],
                   CSTG.rearrange("r (p k) -> r k p", k=4)[0:17, tap], IF[0:17, 0:17])
                CP(k, CW[:, tap * 17:(tap + 1) * 17], CVP[:, 0:17])
            k.dma(CSTG[0:17, 0:128], A("convb").rearrange("(r c) -> r c", c=128))
            TR(k, CVP[:, 0:17], CSTG[0:17, 0:128], IF[0:17, 0:17])
            CP(k, CB, CVP[:, 0:17])
            dtbc = colsl("dtb", 1)
            for hb in range(T // HB):
                t0 = hb * HB
                for c in range(17):
                    if hb == 0:
                        MEMSET(k, XIN[:, c * 259:c * 259 + 3], 0.0)
                        k.dma(XIN[:, c * 259 + 3:c * 259 + 259],
                              A("zx")[2048 + c * 128:2048 + (c + 1) * 128, 0:HB])
                    else:
                        k.dma(XIN[:, c * 259:c * 259 + 259],
                              A("zx")[2048 + c * 128:2048 + (c + 1) * 128, t0 - 3:t0 + HB])
                for c in range(17):
                    x0 = XIN[:, c * 259:c * 259 + 256]
                    TS(k, ACC, x0, CW[:, 0 * 17 + c:0 * 17 + c + 1], OP.mult,
                       CB[:, c:c + 1], OP.add)
                    for tap in range(1, 4):
                        xk = XIN[:, c * 259 + tap:c * 259 + tap + 256]
                        STT(k, ACC, xk, CW[:, tap * 17 + c:tap * 17 + c + 1], ACC,
                            OP.mult, OP.add)
                    ACT(k, SIL[:, c * 256:(c + 1) * 256], ACC, AF.Silu)
                    k.dma(A("xbcs")[c * 128:(c + 1) * 128, t0:t0 + HB],
                          SIL[:, c * 256:(c + 1) * 256])
                k.dma(DTI, A("zx")[4224:4256, t0:t0 + HB])
                TS(k, DTI, DTI, dtbc[0:32, :], OP.add)
                TS(k, DW2, DTI, -1.0, OP.mult)
                TT(k, DW, DTI, DW2, OP.max)
                ACT(k, DW, DW, AF.Exp, scale=-1.0)
                ACT(k, DW, DW, AF.Ln, bias=1.0)
                ACT(k, DW2, DTI, AF.Relu)
                TT(k, DW, DW, DW2, OP.add)
                for half in range(2):
                    TR(k, CVPT[:, 0:32], DW[0:32, half * 128:(half + 1) * 128],
                       IF[0:32, 0:32])
                    CP(k, ACC[:, 0:32], CVPT[:, 0:32])
                    k.dma(A("dts")[t0 + half * 128:t0 + (half + 1) * 128, 0:32],
                          ACC[:, 0:32])
            k.stage_end("convdt")
            _es.close()

    def scan_stage2():
        _es = ExitStack()
        XS = _es.enter_context(_sbt("scx", [128, 17 * 512], BF))
        C0 = _es.enter_context(_sbt("scc0", [128, 128], BF))
        DTT = _es.enter_context(_sbt("scdt", [128, 4 * 32], F32))
        S = _es.enter_context(_sbt("scS", [128, 32], F32))
        ES = _es.enter_context(_sbt("sceS", [128, 32], F32))
        AREP = _es.enter_context(_sbt("scA", [128, 32], F32))
        ATOK = _es.enter_context(_sbt("scar", [128, 32], F32))
        DREP = _es.enter_context(_sbt("scd64", [128, 32], F32))
        GS = _es.enter_context(_sbt("scg", [128, 128], F32))
        BT = _es.enter_context(_sbt("scbt", [128, 64], BF))
        VE = _es.enter_context(_sbt("scv", [128, 128], F32))
        VEX = _es.enter_context(_sbt("scve", [128, 128], F32))
        GW = _es.enter_context(_sbt("scgw", [128, 128], BF))
        SCT = _es.enter_context(_sbt("scsc", [128, 128], F32))
        ECT = _es.enter_context(_sbt("scec", [128, 64], F32))
        XP = _es.enter_context(_sbt("scxp", [128, 64], BF))
        XPP = _es.enter_context(_sbt("scxpp", [128, 64], BF))
        CT = _es.enter_context(_sbt("scct", [128, 128], BF))
        DA = _es.enter_context(_sbt("scda", [128, 2], F32))
        H32 = _es.enter_context(_sbt("sch32", [64, 4096], F32))
        HBF = _es.enter_context(_sbt("schbf", [64, 2048], BF))
        Y = _es.enter_context(_sbt("scy", [128, 16 * 512], BF))
        ROW1 = _es.enter_context(_sbt("scrow", [1, 32], F32))
        PS_S = _es.enter_context(_pst("scps_s", [128, 32], F32))
        PS_ROW = _es.enter_context(_pst("scps_row", [128, 128], F32))
        PS_XT = _es.enter_context(_pst("scps_xt", [128, 64], BF))
        PS_Y = _es.enter_context(_pst("scps_y", [128, 128], F32))
        PS_EC = _es.enter_context(_pst("scps_ec", [128, 128], F32))
        PS_HD = _es.enter_context(_pst("scps_hd", [128, 64], F32))
        PS_G = _es.enter_context(_pst("scps_g", [128, 128], F32))
        PS_BT = _es.enter_context(_pst("scps_bt", [128, 128], BF))
        if True:
            k.dma(ROW1, A("alog")[0:32][None, :])
            ACT(k, ROW1, ROW1, AF.Exp)
            SMUL(k, ROW1, ROW1, -1.0)
            MM(k, PS_S, ONESF[0:1, 0:128], ROW1, True, True)
            CP(k, AREP, PS_S)
            k.dma(ROW1, A("Dh")[0:32][None, :])
            MM(k, PS_S, ONESF[0:1, 0:128], ROW1, True, True)
            CP(k, DREP, PS_S)
            MEMSET(k, H32[0:64, 0:4096], 0.0)
            MEMSET(k, HBF[0:64, 0:2048], 0.0)
            for b in range(NB):
                t0 = b * TB
                for c in range(17):
                    k.dma(XS[:, c * 512:(c + 1) * 512],
                          A("xbcs")[c * 128:(c + 1) * 128, t0:t0 + TB])
                for cc in range(4):
                    k.dma(DTT[:, cc * 32:(cc + 1) * 32],
                          A("dts")[t0 + cc * 128:t0 + (cc + 1) * 128, 0:32])
                for cc in range(4):
                    ch = b * 4 + cc
                    pg = ch % 2
                    dtc = DTT[:, cc * 32:(cc + 1) * 32]
                    TT(k, ATOK, dtc, AREP, OP.mult)
                    MM(k, PS_S, UTF, ATOK, True, True)
                    CP(k, S, PS_S)
                    ACT(k, ES, S, AF.Exp)
                    Bcm = XS[0:64, 16 * 512 + cc * 128:16 * 512 + (cc + 1) * 128]
                    Ccm = XS[64:128, 16 * 512 + cc * 128:16 * 512 + (cc + 1) * 128]
                    TR(k, PS_XT, Ccm, ID2B[64:128, 0:64])
                    CP(k, XP, PS_XT)
                    TR(k, PS_BT[0:64, 0:128], XP, IB)
                    CP(k, C0[0:64, :], PS_BT[0:64, 0:128])
                    MM(k, PS_G, Bcm, C0[0:64, :], True, True)
                    CP(k, GS, PS_G)
                    TR(k, PS_BT[:, 0:64], Bcm, ID2B[0:64, 0:64])
                    CP(k, BT, PS_BT[:, 0:64])
                    for h in range(32):
                        pb_ = 64 * (h % 2)
                        hsl = slice(pb_, pb_ + 64)
                        tp = None if pb_ == 0 else (0, 64)
                        Xcm = XS[hsl, (h // 2) * 512 + cc * 128:(h // 2) * 512 + (cc + 1) * 128]
                        hcol = h * 64
                        hold = H32[0:64, pg * 2048 + hcol:pg * 2048 + hcol + 64]
                        hnew = H32[0:64, (1 - pg) * 2048 + hcol:(1 - pg) * 2048 + hcol + 64]
                        hbf = HBF[0:64, hcol:hcol + 64]
                        TS(k, SCT, ONESF[:, 0:128], S[:, h:h + 1], OP.mult)
                        TR(k, PS_ROW, SCT, IF)
                        STT(k, VE, PS_ROW, S[:, h:h + 1], LTM, OP.subtract, OP.add)
                        ACT(k, VEX, VE, AF.Exp)
                        TT(k, GW, VEX, GS, OP.mult)
                        TR(k, PS_XT, Xcm, ID2B[hsl, 0:64])
                        TS(k, XP, PS_XT, dtc[:, h:h + 1], OP.mult)
                        TS(k, XPP, XP, VEX[:, 127:128], OP.mult)
                        MM(k, PS_Y[hsl, :], XP, GW, True, False, tp=tp)
                        TS(k, ECT, ONESF[:, 0:64], ES[:, h:h + 1], OP.mult)
                        TR(k, PS_EC[0:64, :], ECT, IF)
                        TT(k, CT[0:64, :], C0[0:64, :], PS_EC[0:64, :], OP.mult)
                        CP(k, DA[0:64, 0:1], PS_EC[0:64, 127:128])
                        MM(k, PS_Y[hsl, :], hbf, CT[0:64, :], False, True, tp=tp)
                        ysl = Y[hsl, (h // 2) * 512 + cc * 128:(h // 2) * 512 + (cc + 1) * 128]
                        STT(k, ysl, Xcm, DREP[hsl, h:h + 1], PS_Y[hsl, :], OP.mult, OP.add)
                        MM(k, PS_HD[0:64, 0:64], BT, XPP, True, True)
                        STT(k, hnew, hold, DA[0:64, 0:1], PS_HD[0:64, 0:64], OP.mult, OP.add)
                        CP(k, hbf, hnew)
                for c in range(16):
                    k.dma(A("ymm")[c * 128:(c + 1) * 128, t0:t0 + TB],
                          Y[:, c * 512:(c + 1) * 512])
            k.stage_end("scan")
            _es.close()

    def gate_stage():
        HB = 256
        _es = ExitStack()
        YB = _es.enter_context(_sbt("gty", [128, 16 * 256], BF))
        Z = _es.enter_context(_sbt("gtz", [128, 16 * 256], F32))
        YG = _es.enter_context(_sbt("gtyg", [128, 16 * 256], F32))
        SQ = _es.enter_context(_sbt("gtsq", [128, 256], BF))
        ST = _es.enter_context(_sbt("gtst", [1, 256], F32))
        RB = _es.enter_context(_sbt("gtrb", [1, 256], BF))
        O = _es.enter_context(_sbt("gto", [128, 256], BF))
        P1 = _es.enter_context(_pst("gtp1", [1, 256], F32))
        P2 = _es.enter_context(_pst("gtp2", [128, 256], F32))
        if True:
            nw = colsl("normw", 16)
            for hb in range(T // HB):
                t0 = hb * HB
                for c in range(16):
                    k.dma(YB[:, c * 256:(c + 1) * 256],
                          A("ymm")[c * 128:(c + 1) * 128, t0:t0 + HB])
                    k.dma(Z[:, c * 256:(c + 1) * 256],
                          A("zx")[c * 128:(c + 1) * 128, t0:t0 + HB])
                for c in range(16):
                    zc = Z[:, c * 256:(c + 1) * 256]
                    ACT(k, zc, zc, AF.Silu)
                    TT(k, YG[:, c * 256:(c + 1) * 256], YB[:, c * 256:(c + 1) * 256],
                       zc, OP.mult)
                for c in range(16):
                    ACT(k, SQ, YG[:, c * 256:(c + 1) * 256], AF.Square)
                    MM(k, P1, ONESB[:, 0:1], SQ, c == 0, c == 15)
                TS(k, ST, P1[0:1, :], 1.0 / 2048.0, OP.mult)
                ACT(k, ST, ST, AF.Sqrt, bias=EPS)
                RECIP(k, RB, ST)
                MM(k, P2, ONESB[0:1, 0:128], RB, True, True)
                for c in range(16):
                    STT(k, O, YG[:, c * 256:(c + 1) * 256], nw[:, c:c + 1], P2,
                        OP.mult, OP.mult)
                    k.dma(A("yn")[c * 128:(c + 1) * 128, t0:t0 + HB], O)
            k.stage_end("gate")
            _es.close()

    def outproj_stage():
        _es = ExitStack()
        W0 = _es.enter_context(_sbt("opw0", [128, 16 * 1024], BF))
        STG = _es.enter_context(_sbt("opstg", [128, 2048], F32))
        STGB = _es.enter_context(_sbt("opstgb", [128, 2048], BF))
        YN = _es.enter_context(_sbt("opy", [128, 16 * 512], BF))
        RS = _es.enter_context(_sbt("opr", [128, 4096], F32))
        O = _es.enter_context(_sbt("opo", [128, 512], F32))
        PS = _es.enter_context(_pst("opp", [128, 512], F32))
        if True:
            load_w_T(W0, A("outpw"), 1024, 2048, STG, STGB)
            for b in range(NB):
                t0 = b * TB
                for c in range(16):
                    k.dma(YN[:, c * 512:(c + 1) * 512],
                          A("yn")[c * 128:(c + 1) * 128, t0:t0 + TB])
                for c in range(8):
                    k.dma(RS[:, c * 512:(c + 1) * 512],
                          A("t0")[c * 128:(c + 1) * 128, t0:t0 + TB])
                for m in range(8):
                    for c in range(16):
                        MM(k, PS, W0[:, c * 1024 + m * 128:c * 1024 + (m + 1) * 128],
                           YN[:, c * 512:(c + 1) * 512], c == 0, c == 15)
                    TT(k, O, PS, RS[:, m * 512:(m + 1) * 512], OP.add)
                    k.dma(A("t0")[m * 128:(m + 1) * 128, t0:t0 + TB], O)
            k.stage_end("outproj")
            _es.close()

    def mlp1_stage(half):
        r0 = half * 2048
        _es = ExitStack()
        W0 = _es.enter_context(_sbt("m1w0", [128, 8 * 2048], BF))
        STG = _es.enter_context(_sbt("m1stg", [128, 1024], F32))
        STGB = _es.enter_context(_sbt("m1stgb", [128, 1024], BF))
        XIN = _es.enter_context(_sbt("m1x", [128, 4096], F32))
        XB = _es.enter_context(_sbt("m1xb", [128, 4096], BF))
        O = _es.enter_context(_sbt("m1o", [128, 512], BF))
        PS = _es.enter_context(_pst("m1p", [128, 512], F32))
        if True:
            with _pst("m1jt", [128, 512], BF) as JP:
                for m in range(16):
                    k.dma(STG[:, 0:1024], A("mw1")[r0 + m * 128:r0 + (m + 1) * 128, 0:1024])
                    CP(k, STGB[:, 0:1024], STG[:, 0:1024])
                    for c in range(8):
                        pslot = c % 4
                        TR(k, JP[:, pslot * 128:(pslot + 1) * 128],
                           STGB[:, c * 128:(c + 1) * 128], IB)
                        CP(k, W0[:, c * 2048 + m * 128:c * 2048 + (m + 1) * 128],
                           JP[:, pslot * 128:(pslot + 1) * 128])
                mb1 = colsl("mb1", 32)
                for b in range(NB):
                    t0 = b * TB
                    for c in range(8):
                        k.dma(XIN[:, c * 512:(c + 1) * 512],
                              A("t1")[c * 128:(c + 1) * 128, t0:t0 + TB])
                    CP(k, XB, XIN)
                    for m in range(16):
                        for c in range(8):
                            MM(k, PS, W0[:, c * 2048 + m * 128:c * 2048 + (m + 1) * 128],
                               XB[:, c * 512:(c + 1) * 512], c == 0, c == 7)
                        ACT(k, O, PS, AF.Gelu, bias=mb1[:, half * 16 + m:half * 16 + m + 1])
                        k.dma(A("m1d")[r0 + m * 128:r0 + (m + 1) * 128, t0:t0 + TB], O)
            k.stage_end("mlp1_%d" % half)
            _es.close()

    def mlp2_stage(half):
        r0 = half * 2048
        _es = ExitStack()
        W0 = _es.enter_context(_sbt("m2w0", [128, 16 * 1024], BF))
        STG = _es.enter_context(_sbt("m2stg", [128, 2048], F32))
        STGB = _es.enter_context(_sbt("m2stgb", [128, 2048], BF))
        XB = _es.enter_context(_sbt("m2x", [128, 16 * 512], BF))
        RS = _es.enter_context(_sbt("m2r", [128, 4096], F32))
        O = _es.enter_context(_sbt("m2o", [128, 512], F32))
        PS = _es.enter_context(_pst("m2p", [128, 512], F32))
        if True:
            with _pst("m2jt", [128, 512], BF) as JP:
                for m in range(8):
                    k.dma(STG[:, 0:2048],
                          A("mw2")[m * 128:(m + 1) * 128, r0:r0 + 2048])
                    CP(k, STGB[:, 0:2048], STG[:, 0:2048])
                    for c in range(16):
                        pslot = c % 4
                        TR(k, JP[:, pslot * 128:(pslot + 1) * 128],
                           STGB[:, c * 128:(c + 1) * 128], IB)
                        CP(k, W0[:, c * 1024 + m * 128:c * 1024 + (m + 1) * 128],
                           JP[:, pslot * 128:(pslot + 1) * 128])
                mb2 = colsl("mb2", 8)
                for b in range(NB):
                    t0 = b * TB
                    for c in range(16):
                        k.dma(XB[:, c * 512:(c + 1) * 512],
                              A("m1d")[r0 + c * 128:r0 + (c + 1) * 128, t0:t0 + TB])
                    if half == 1:
                        for c in range(8):
                            k.dma(RS[:, c * 512:(c + 1) * 512],
                                  A("tp")[c * 128:(c + 1) * 128, t0:t0 + TB])
                    for m in range(8):
                        for c in range(16):
                            MM(k, PS, W0[:, c * 1024 + m * 128:c * 1024 + (m + 1) * 128],
                               XB[:, c * 512:(c + 1) * 512], c == 0, c == 15)
                        if half == 0:
                            CP(k, O, PS)
                        else:
                            STT(k, O, PS, mb2[:, m:m + 1],
                                RS[:, m * 512:(m + 1) * 512], OP.add, OP.add)
                        k.dma(A("tp")[m * 128:(m + 1) * 128, t0:t0 + TB], O)
            k.stage_end("mlp2_%d" % half)
            _es.close()

    def mlp2b_addt0():
        axpy_pass("mlpres", A("t0"), A("tp"), A("t0"), 1.0)

    def ffn_module(src_res):
        cur = "u"
        oth = "t0"
        for i in range(6):
            attn_stage(i, A(cur), A(oth))
            ffn1_stage(i, A(oth), 0)
            ffn1_stage(i, A(oth), 1)
            ffn2_stage(i, A(oth), A(cur))
        return cur

    ln_pass("lnA", A("x_cm"), A("u"))
    fin = ffn_module(None)
    axpy_pass("resC", A("x_cm"), A(fin), A("xr"), 0.5)
    ln_pass("lnD", A("xr"), A("t0"))
    ln_pass("lnE", A("t0"), A("t1"), colsl("ln1w", 8), colsl("ln1b", 8))
    inproj_stage(0)
    inproj_stage(1)
    convdt_stage()
    scan_stage2()
    gate_stage()
    outproj_stage()
    ln_pass("lnF6", A("t0"), A("t1"), colsl("ln2w", 8), colsl("ln2b", 8))
    mlp1_stage(0)
    mlp1_stage(1)
    mlp2_stage(0)
    mlp2_stage(1)
    mlp2b_addt0()
    axpy_pass("resF9", A("xr"), A("t0"), A("xr"), 1.0)
    ln_pass("lnG", A("xr"), A("u"))
    fin = ffn_module(None)
    axpy_pass("resI", A("xr"), A(fin), A("out_cm"), 0.5)

    blk = nc.Block("fin")
    with blk:
        cnts = dict(k.cnt)
        sems = dict(k.sem)
        qd = k.q["d"]

        def _fin(E, qd=qd, cnts=cnts, sems=sems):
            for f in qd:
                f(E)
            for o in "tvs":
                if cnts[o]:
                    E.wait_ge(sems[o], cnts[o])
            E.wait_ge(sems["d"], cnts["d"])
        blk.sync(_fin)
        for e, attr in (("t", "tensor"), ("v", "vector"), ("s", "scalar")):
            ops = k.q[e]
            if ops:
                getattr(blk, attr)(lambda E, ops=ops: [f(E) for f in ops])
    return nc


_NC_CACHE = [None]


def kernel(**inputs):
    import ml_dtypes
    if _NC_CACHE[0] is None:
        nc = bass.Bass("TRN2", target_bir_lowering=False)
        build(nc)
        _NC_CACHE[0] = nc
    nc = _NC_CACHE[0]
    consts = _host_consts()
    remap = {
        "ln1_w": "L0_attn_ln1_w", "ln1_b": "L0_attn_ln1_b",
        "inpw": "L0_attn_in_proj_w", "convw": "L0_attn_conv_w",
        "convb": "L0_attn_conv_b", "dtb": "L0_attn_dt_bias",
        "alog": "L0_attn_A_log", "Dh": "L0_attn_D", "normw": "L0_attn_norm_w",
        "outpw": "L0_attn_out_proj_w", "ln2_w": "L0_attn_ln2_w",
        "ln2_b": "L0_attn_ln2_b", "mw1": "L0_attn_mlp_w1", "mb1": "L0_attn_mlp_b1",
        "mw2": "L0_attn_mlp_w2", "mb2": "L0_attn_mlp_b2",
        "fqkv": "L0_ffn_Wqkv", "fqs": "L0_ffn_q_scale", "fks": "L0_ffn_k_scale",
        "fwo": "L0_ffn_Wo", "flg": "L0_ffn_ln_g", "flb": "L0_ffn_ln_b",
        "fw1": "L0_ffn_W1", "fb1": "L0_ffn_b1", "fw2": "L0_ffn_W2",
        "fb2": "L0_ffn_b2",
    }
    base = {}
    for nm, src in remap.items():
        base[nm] = np.ascontiguousarray(inputs[src].astype(np.float32))
    base.update(consts)
    x = inputs["L0_attn_x"]
    in_maps = []
    for b in range(8):
        m = dict(base)
        m["x_cm"] = np.ascontiguousarray(x[b].T.astype(np.float32))
        in_maps.append(m)
    res = run_bass_kernel_spmd(nc, in_maps, core_ids=list(range(8)))
    out = np.stack([np.asarray(r["out_cm"]).T for r in res.results], axis=0)
    return out.astype(np.float32)
